# version 16
# baseline (speedup 1.0000x reference)
import os
import numpy as np
import ml_dtypes
import concourse.bass as bass
import concourse.mybir as mybir
from concourse.bass_utils import run_bass_kernel_spmd
from contextlib import ExitStack

F32 = mybir.dt.float32
BF16 = mybir.dt.bfloat16
AF = mybir.ActivationFunctionType
ALU = mybir.AluOpType
AX = mybir.AxisListType

TOK = 2048
D = 2048
E = 8192
H = 16
NCORES = 8
NT = TOK // 128
NF = D // 128
NQ = 4
QT = TOK // NQ
HT = TOK // 2
KG = 1024
NKG = E // KG
LN_EPS = 1e-5


class Sem:
    def __init__(self, es, nc, name, owner):
        self.h = es.enter_context(nc.semaphore(name))
        self.owner = owner
        self.n = 0


class Prog:

    def __init__(self):
        self.events = []
        self.own = {}

    def emit(self, engine, fn, waits=(), incs=()):
        incs = list(incs)
        is_dma = any(k >= 16 for (_, k) in incs)
        auto = self.own.get(engine) if engine in ("vector", "scalar", "gpsimd") else None
        w = []
        if auto is not None and auto.n > 0:
            w.append((auto.h, auto.n))
        for s in waits:
            if isinstance(s, tuple):
                s, v = s
            else:
                v = s.n
            if v > 0:
                w.append((s.h, v))
        if auto is not None and not is_dma and not any(s is auto for (s, _) in incs):
            incs.append((auto, 1))
        for (s, k) in incs:
            assert s.owner == engine, f"sem owned by {s.owner}, inc from {engine}"
            s.n += k
        self.events.append((engine, fn, w, [(s.h, k) for (s, k) in incs]))

    def run_engine(self, engine_name, eng):
        for (en, fn, waits, incs) in self.events:
            if en != engine_name:
                continue
            for (sh, v) in waits:
                eng.wait_ge(sh, v)
            inst = fn(eng)
            for (sh, k) in incs:
                inst.then_inc(sh, k)


def build_program():
    nc = bass.Bass("TRN2", target_bir_lowering=False)

    x_h = nc.dram_tensor("x", [TOK, D], F32, kind="ExternalInput").ap()
    wq_h = nc.dram_tensor("wq", [D, D], F32, kind="ExternalInput").ap()
    wv_h = nc.dram_tensor("wv", [D, D], F32, kind="ExternalInput").ap()
    wg_h = nc.dram_tensor("wg", [D, D], F32, kind="ExternalInput").ap()
    wo_h = nc.dram_tensor("wo", [D, D], F32, kind="ExternalInput").ap()
    beta_h = nc.dram_tensor("beta", [1, H], F32, kind="ExternalInput").ap()
    lngr_h = nc.dram_tensor("lngr", [1, D], F32, kind="ExternalInput").ap()
    lnbr_h = nc.dram_tensor("lnbr", [1, D], F32, kind="ExternalInput").ap()
    wuT_h = nc.dram_tensor("wuT", [D, E], F32, kind="ExternalInput").ap()
    wgT_h = nc.dram_tensor("wgT", [D, E], F32, kind="ExternalInput").ap()
    wdT_h = nc.dram_tensor("wdT", [E, D], F32, kind="ExternalInput").ap()
    emat_h = nc.dram_tensor("emat", [128, 128], F32, kind="ExternalInput").ap()
    mlo_h = nc.dram_tensor("mlo", [128, 128], BF16, kind="ExternalInput").ap()
    mhi_h = nc.dram_tensor("mhi", [128, 128], BF16, kind="ExternalInput").ap()
    out_h = nc.dram_tensor("out", [TOK, D], F32, kind="ExternalOutput").ap()

    x1tm_h = nc.dram_tensor("x1tm", [TOK, D], BF16, kind="Internal").ap()
    a_h = nc.dram_tensor("a_scr", [D, TOK], BF16, kind="Internal").ap()
    r1tm_h = nc.dram_tensor("r1tm", [TOK, D], F32, kind="Internal").ap()
    x2tm_h = nc.dram_tensor("x2tm", [TOK, D], BF16, kind="Internal").ap()
    tu_h = nc.dram_tensor("tu", [D, E], BF16, kind="Internal").ap()
    tg_h = nc.dram_tensor("tg", [D, E], BF16, kind="Internal").ap()
    td_h = nc.dram_tensor("td", [E, D], BF16, kind="Internal").ap()
    d_scr_h = nc.dram_tensor("d_scr", [D, TOK], BF16, kind="Internal").ap()
    KDBG = os.environ.get("KDBG", "0") == "1"
    if KDBG:
        dbg_h = {nm: nc.dram_tensor(nm, [D, TOK], BF16, kind="Internal").ap()
                 for nm in ("dbgv", "dbg2", "dbg3", "dbg4", "dbg5")}

    with ExitStack() as es:
        sbuf = lambda name, n_bf16: es.enter_context(nc.sbuf_tensor(name, [128, n_bf16], BF16))
        poolA = sbuf("poolA", 17 * TOK)
        poolB = sbuf("poolB", 8 * D)
        poolC = sbuf("poolC", 8 * D)
        poolE = sbuf("poolE", 12 * D)
        sqdump = sbuf("sqdump", D)

        arena = poolA
        accf = poolA[:, 0:2 * NF * HT].bitcast(F32)
        silbuf = poolA[:, 2 * NF * HT:2 * NF * HT + HT]
        wsb = poolB
        uslab = poolB
        dstage = poolB
        pin = poolC[:, 0:4 * D].bitcast(F32)
        pc1 = poolC[:, 4 * D:5 * D]
        pc2 = poolC[:, 5 * D:6 * D]
        pto = poolC[:, 6 * D:8 * D]
        gslab = poolC
        a6stage = poolE[:, 0:2 * D]
        x1q = poolE[:, 0:NF * QT]
        hbuf = poolE[:, 0:(KG // 128) * HT]
        x2h = poolE[:, 4 * D:4 * D + NF * HT]
        tma = poolE[:, 0:2 * D]
        tmx = poolE[:, 2 * D:6 * D].bitcast(F32)
        tmo = poolE[:, 6 * D:10 * D].bitcast(F32)
        tms = poolE[:, 10 * D:12 * D]
        yrow = poolE[0:1, 0:8 * D].bitcast(F32)
        lnscr = poolE[:, 8 * D:9 * D]
        lnscr2 = poolE[:, 9 * D:10 * D]

        sb2 = lambda name, shape, dt: es.enter_context(nc.sbuf_tensor(name, shape, dt))
        lamb = sb2("lamb", [128, H], F32)
        lngr = poolE[0:1, 0:2 * D].bitcast(F32)
        lnbr = poolE[0:1, 2 * D:4 * D].bitcast(F32)
        ones1 = sb2("ones1", [1, 128], F32)
        ones128 = sb2("ones128", [128, 1], BF16)
        ones128f = sb2("ones128f", [128, 1], F32)
        epsb = sb2("epsb", [128, 1], F32)
        mhalf = sb2("mhalf", [128, 1], F32)
        c128b = sb2("c128b", [128, 1], F32)
        taub = sb2("taub", [128, 3], F32)
        ntaub = sb2("ntaub", [128, 3], F32)
        su2b = sb2("su2b", [128, 1], F32)
        psums = sb2("psums", [128, 3 * 64], F32)
        pdump = sbuf("pdump", D)
        srow = sb2("srow", [128, 12], F32)
        rowbuf = sb2("rowbuf", [1, 128], F32)
        brow = sb2("brow", [1, H], F32)
        emat = sb2("emat_s", [128, 128], F32)
        mlo = sb2("mlo_s", [128, 128], BF16)
        mhi = sb2("mhi_s", [128, 128], BF16)
        tgen = sb2("tgen", [128, 256], BF16)
        tgf = sb2("tgf", [128, 128], F32)

        lnl = sb2("lnl", [128, H], F32)
        l128 = sb2("l128", [128, H], F32)
        lnl128 = sb2("lnl128", [128, H], F32)
        psA = es.enter_context(nc.psum_tensor("psA", [128, D], F32))
        psB = es.enter_context(nc.psum_tensor("psB", [128, D], F32))

        mks = lambda nm, ow: Sem(es, nc, nm, ow)
        dSM = mks("dSM", "sync")
        dL0 = mks("dL0", "sync")
        dL1 = mks("dL1", "sync")
        dX1 = mks("dX1", "sync")
        dXQ = mks("dXQ", "sync")
        dXQb = mks("dXQb", "sync")
        dAW = mks("dAW", "sync")
        dR1 = mks("dR1", "sync")
        dX2 = mks("dX2", "sync")
        dXH = mks("dXH", "sync")
        dDW = mks("dDW", "sync")
        dOU = mks("dOU", "sync")
        dU = mks("dU", "sync")
        dGS = mks("dGS", "sync")
        dP0 = mks("dP0", "gpsimd")
        dP1 = mks("dP1", "gpsimd")
        dG = mks("dG", "gpsimd")
        dA = mks("dA", "scalar")
        dDB = mks("dDB", "sync")
        dYW = mks("dYW", "sync")
        dYR = mks("dYR", "sync")
        ct = mks("ct", "tensor")
        cv = mks("cv", "vector")
        ca = mks("ca", "scalar")
        cp = mks("cp", "gpsimd")
        ALL = [dSM, dL0, dL1, dX1, dXQ, dXQb, dAW, dR1, dX2, dXH, dDW, dOU,
               dU, dGS, dP0, dP1, dG, dA, dDB, dYW, dYR, ct, cv, ca, cp]

        block = es.enter_context(nc.Block())
        pr = Prog()
        pr.own = {"vector": cv, "scalar": ca, "gpsimd": cp}

        def snap():
            return [(s, s.n) for s in ALL]

        pr.emit("vector", lambda e: e.memset(ones1[:, :], 1.0), incs=[(cv, 1)])
        pr.emit("vector", lambda e: e.memset(ones128[:, :], 1.0), incs=[(cv, 1)])
        pr.emit("vector", lambda e: e.memset(ones128f[:, :], 1.0), incs=[(cv, 1)])
        pr.emit("vector", lambda e: e.memset(epsb[:, :], LN_EPS), incs=[(cv, 1)])
        pr.emit("vector", lambda e: e.memset(mhalf[:, :], -0.5), incs=[(cv, 1)])
        pr.emit("vector", lambda e: e.memset(c128b[:, :], 128.0), incs=[(cv, 1)])
        pr.emit("sync", lambda e: e.dma_start(brow[:, :], beta_h), incs=[(dSM, 16)])

        pr.emit("sync", lambda e: e.dma_start(emat[:, :], emat_h), incs=[(dSM, 16)])
        pr.emit("sync", lambda e: e.dma_start(mlo[:, :], mlo_h), incs=[(dSM, 16)])
        pr.emit("sync", lambda e: e.dma_start(mhi[:, :], mhi_h), incs=[(dSM, 16)])
        pr.emit("scalar", lambda e: e.activation(brow[:, :], brow[:, :], AF.Sigmoid),
                waits=[dSM], incs=[(ca, 1)])
        pr.emit("tensor", lambda e: e.matmul(psB[:, 0:H], ones1[:, :], brow[:, :],
                                             start=True, stop=True),
                waits=[ca, cv], incs=[(ct, 1)])
        pr.emit("vector", lambda e: e.tensor_copy(lamb[:, :], psB[:, 0:H]),
                waits=[ct], incs=[(cv, 1)])
        pr.emit("scalar", lambda e: e.activation(lnl[:, :], lamb[:, :], AF.Ln),
                waits=[cv], incs=[(ca, 1)])
        pr.emit("scalar", lambda e: e.activation(l128[:, :], lnl[:, :], AF.Exp, scale=c128b[:, :]),
                incs=[(ca, 1)])

        def ln_tm_chunk(src, dst, rs, wb, extra_waits=()):
            s = src
            pr.emit("scalar", (lambda s, rs: lambda e: e.activation(
                sqdump[:, :], s, AF.Square, accum_out=srow[:, rs:rs + 1]))(s, rs),
                    waits=list(extra_waits) + [cv], incs=[(ca, 1)])
            pr.emit("vector", (lambda s, rs: lambda e: e.reduce_sum(
                srow[:, rs + 1:rs + 2], s, axis=AX.X))(s, rs),
                    waits=list(extra_waits))
            pr.emit("vector", (lambda rs: lambda e: e.tensor_scalar(
                srow[:, rs + 2:rs + 3], srow[:, rs + 1:rs + 2], 1.0 / D, None, ALU.mult))(rs))
            pr.emit("vector", (lambda rs: lambda e: e.tensor_mul(
                srow[:, rs + 3:rs + 4], srow[:, rs + 2:rs + 3], srow[:, rs + 2:rs + 3]))(rs))
            pr.emit("vector", (lambda rs: lambda e: e.scalar_tensor_tensor(
                srow[:, rs + 4:rs + 5], srow[:, rs:rs + 1], 1.0 / D,
                srow[:, rs + 3:rs + 4], ALU.mult, ALU.subtract))(rs),
                    waits=[ca], incs=[(cv, 1)])
            pr.emit("scalar", (lambda rs: lambda e: e.activation(
                srow[:, rs + 5:rs + 6], srow[:, rs + 4:rs + 5], AF.Ln, bias=epsb[:, :]))(rs),
                    waits=[cv])
            pr.emit("scalar", (lambda rs: lambda e: e.activation(
                srow[:, rs + 5:rs + 6], srow[:, rs + 5:rs + 6], AF.Exp, scale=mhalf[:, :]))(rs),
                    incs=[(ca, 1)])
            pr.emit("vector", (lambda s, dst, rs: lambda e: e.tensor_scalar(
                dst, s, srow[:, rs + 2:rs + 3], srow[:, rs + 5:rs + 6],
                ALU.subtract, ALU.mult))(s, dst, rs),
                    waits=[ca, wb], incs=[(cv, 1)])

        prep_srcs = [(wuT_h, D, E), (wgT_h, D, E), (wdT_h, E, D)]
        slot_seen = set()

        def gen_p1():
            amarks = []
            ti = 0
            for mi, (wsrc, R, C) in enumerate(prep_srcs):
                nr, ncq = R // 128, C // D
                for t in range(nr * ncq):
                    rt, cq = t // ncq, t % ncq
                    b = t % 2
                    dst = pin[:, b * D:(b + 1) * D]
                    w = [amarks[ti - 2]] if ti >= 2 else []
                    pr.emit("gpsimd", (lambda wsrc, rt, cq, dst: lambda e: e.dma_start(
                        dst, wsrc[rt * 128:(rt + 1) * 128, cq * D:(cq + 1) * D]))(wsrc, rt, cq, dst),
                            waits=w, incs=[(dP0 if b == 0 else dP1, 16)])
                    slot = psums[:, mi * 64 + t: mi * 64 + t + 1]
                    dP = dP0 if b == 0 else dP1
                    pr.emit("scalar", (lambda slot, dst: lambda e: e.activation(
                        pdump[:, :], dst, AF.Abs, accum_out=slot))(slot, dst),
                            waits=[dP], incs=[(ca, 1)])
                    amarks.append((ca, ca.n))
                    ti += 1
                    yield
            for mi in range(3):
                pr.emit("vector", (lambda mi: lambda e: e.reduce_sum(
                    srow[:, 10:11], psums[:, mi * 64:mi * 64 + 64], axis=AX.X))(mi),
                        waits=[ca, cp, (ct, ct.n)], incs=[(cv, 1)])
                pr.emit("tensor", lambda e: e.matmul(psB[0:1, 1536:1537], ones128f[:, :], srow[:, 10:11],
                                                     start=True, stop=True),
                        waits=[cv], incs=[(ct, 1)])
                pr.emit("vector", (lambda mi: lambda e: e.tensor_scalar(
                    rowbuf[0:1, mi:mi + 1], psB[0:1, 1536:1537], 0.5 / (D * E), 0.5e-8,
                    ALU.mult, ALU.add))(mi), waits=[ct], incs=[(cv, 1)])
            pr.emit("tensor", lambda e: e.matmul(psB[:, 1540:1543], ones1[:, :], rowbuf[0:1, 0:3],
                                                 start=True, stop=True),
                    waits=[cv], incs=[(ct, 1)])
            pr.emit("vector", lambda e: e.tensor_copy(taub[:, :], psB[:, 1540:1543]),
                    waits=[ct], incs=[(cv, 1)])
            pr.emit("vector", lambda e: e.tensor_scalar(ntaub[:, :], taub[:, :], -1.0, None, ALU.mult),
                    incs=[(cv, 1)])
            pr.emit("vector", lambda e: e.tensor_copy(su2b[:, :], taub[:, 0:1]), incs=[(cv, 1)])
            yield

        def gen_p15():
            tern_dsts = [tu_h, tg_h, td_h]
            tmarks = []
            ti = 0
            for mi, (wsrc, R, C) in enumerate(prep_srcs):
                dst_h = tern_dsts[mi]
                nr, ncq = R // 128, C // D
                for t in range(nr * ncq):
                    rt, cq = t // ncq, t % ncq
                    b = t % 2
                    pi = pin[:, b * D:(b + 1) * D]
                    po = pto[:, b * D:(b + 1) * D]
                    tb = taub[:, mi:mi + 1]
                    ntb = ntaub[:, mi:mi + 1]
                    dP = dP0 if b == 0 else dP1
                    w = [tmarks[ti - 2][0], tmarks[ti - 2][1]] if ti >= 2 else [cv]
                    pr.emit("gpsimd", (lambda wsrc, rt, cq, pi: lambda e: e.dma_start(
                        pi, wsrc[rt * 128:(rt + 1) * 128, cq * D:(cq + 1) * D]))(wsrc, rt, cq, pi),
                            waits=w, incs=[(dP, 16)])
                    if ti % 2 == 0:
                        pr.emit("vector", (lambda pi, tb: lambda e: e.tensor_scalar(
                            pc1[:, :], pi, tb, 2.0, ALU.is_gt, ALU.mult))(pi, tb),
                                waits=[dP])
                        pr.emit("vector", (lambda pi, ntb: lambda e: e.tensor_scalar(
                            pc2[:, :], pi, ntb, -2.0, ALU.is_lt, ALU.mult))(pi, ntb))
                        pr.emit("vector", (lambda po: lambda e: e.tensor_add(po, pc1[:, :], pc2[:, :]))(po),
                                waits=[(dA, dA.n)], incs=[(cv, 1)])
                        cvm = (cv, cv.n)
                        pr.emit("scalar", (lambda dst_h, rt, cq, po: lambda e: e.dma_start(
                            dst_h[rt * 128:(rt + 1) * 128, cq * D:(cq + 1) * D], po))(dst_h, rt, cq, po),
                                waits=[cvm], incs=[(dA, 16)])
                        tmarks.append((cvm, cvm, cvm))
                    else:
                        pr.emit("scalar", (lambda pi, ntb: lambda e: e.activation(
                            pc1[:, :], pi, AF.Sign, bias=ntb))(pi, ntb),
                                waits=[dP, (cv, cv.n)])
                        pr.emit("scalar", (lambda pi, tb: lambda e: e.activation(
                            pc2[:, :], pi, AF.Sign, bias=tb))(pi, tb),
                                incs=[(ca, 1)])
                        cam = (ca, ca.n)
                        pr.emit("vector", (lambda po: lambda e: e.tensor_add(po, pc1[:, :], pc2[:, :]))(po),
                                waits=[cam, (dA, dA.n)], incs=[(cv, 1)])
                        cvm = (cv, cv.n)
                        pr.emit("scalar", (lambda dst_h, rt, cq, po: lambda e: e.dma_start(
                            dst_h[rt * 128:(rt + 1) * 128, cq * D:(cq + 1) * D], po))(dst_h, rt, cq, po),
                                waits=[cvm], incs=[(dA, 16)])
                        tmarks.append((cvm, cam, cvm))
                    ti += 1
                    yield

        _g1, _g2 = gen_p1(), gen_p15()

        def pump1(n=1):
            for _ in range(n):
                if next(_g1, "end") == "end":
                    break

        def pump2(n=1):
            for _ in range(n):
                if next(_g2, "end") == "end":
                    break

        def exhaust1():
            for _ in _g1:
                pass

        def exhaust2():
            for _ in _g2:
                pass

        p0_in = [tmx[:, 0:D], tmx[:, D:2 * D], tmo[:, 0:D], tmo[:, D:2 * D]]
        p0_out = [tms[:, 0:D], tms[:, D:2 * D], tma[:, 0:D], tma[:, D:2 * D]]
        p0_marks = []
        for r in range(NT):
            bb = r % 4
            dL = dL0 if bb % 2 == 0 else dL1
            xs = p0_in[bb]
            w = [p0_marks[r - 4]] if r >= 4 else [cv]
            pr.emit("sync", (lambda r, xs: lambda e: e.dma_start(
                xs, x_h[r * 128:(r + 1) * 128, :]))(r, xs), waits=w, incs=[(dL, 16)])
            ln_tm_chunk(xs, p0_out[bb], 0 if r % 2 == 0 else 6, dX1,
                        extra_waits=[dL])
            pr.emit("sync", (lambda r, bb: lambda e: e.dma_start(
                x1tm_h[r * 128:(r + 1) * 128, :], p0_out[bb]))(r, bb),
                    waits=[cv], incs=[(dX1, 16)])
            p0_marks.append((cv, cv.n))
            pump1(2)

        def load_attn_w_half(wh, mh, waits):
            pr.emit("gpsimd", (lambda wh, mh: lambda e: e.dma_start(
                wsb[:, :].rearrange("p (k n) -> p k n", k=NF),
                wh[:, mh * 1024:(mh + 1) * 1024].rearrange("(k p) n -> p k n", p=128)))(wh, mh),
                    waits=waits, incs=[(dG, 16)])

        x1q_bufs = [poolE[:, 0:NF * QT], poolE[:, NF * QT:2 * NF * QT]]

        def load_x1q(buf, bi, tq, waits):
            sem = dXQ if bi == 0 else dXQb
            q = "sync"
            for f in range(NF):
                pr.emit(q, (lambda buf, tq, f: lambda e: e.dma_start_transpose(
                    buf[:, f * QT:(f + 1) * QT],
                    x1tm_h[tq * QT:(tq + 1) * QT, f * 128:(f + 1) * 128]))(buf, tq, f),
                        waits=(list(waits) + [dX1] if f == 0 else []), incs=[(sem, 16)])
            return (sem, sem.n)

        def aq3(tq, m):
            return arena[:, 4 * tq * D:4 * (tq + 1) * D].rearrange(
                "p (r f) -> p r f", r=4)[:, :, m * 128:(m + 1) * 128]

        def ps3(ps):
            return ps.rearrange("p (r f) -> p r f", r=4)

        DBGM = []

        def proj_v_tm(wh, prev_mark, pump=None, pn=0, evac="v"):
            entry = [prev_mark, (ct, ct.n)] + list(DBGM)
            sil_mark = [None]
            ci = 0
            slots = [psA[:, 0:512], psA[:, 512:1024], psB[:, 0:512], psB[:, 512:1024]]
            evac_marks = []
            xls = [None, None]
            xls[0] = load_x1q(x1q_bufs[0], 0, 0, entry)
            for mh in range(2):
                load_attn_w_half(wh, mh, entry if mh == 0 else [(ct, ct.n)])
                wload = (dG, dG.n)
                for tq in range(NQ):
                    g = mh * NQ + tq
                    xb = g % 2
                    xload = xls[xb]
                    nmh, ntq = (mh, tq + 1) if tq + 1 < NQ else (mh + 1, 0)
                    if nmh < 2:
                        gw = [(ct, ct.n)]
                        xls[1 - xb] = load_x1q(x1q_bufs[1 - xb], 1 - xb, ntq, gw)
                    for cc in range(4):
                        r = tq * 4 + cc
                        for fg in range(2):
                            ps = slots[ci % 4]
                            x1b = x1q_bufs[xb]
                            for k in range(NF):
                                w = []
                                if k == 0:
                                    w = [wload, xload]
                                    if ci < 4:
                                        w += entry
                                    else:
                                        w.append(evac_marks[ci - 4])
                                pr.emit("tensor", (lambda ps, k, cc, fg, x1b: lambda e: e.matmul(
                                    ps, x1b[:, k * QT + cc * 128: k * QT + cc * 128 + 128],
                                    wsb[:, k * 1024 + fg * 512: k * 1024 + (fg + 1) * 512],
                                    start=(k == 0), stop=(k == NF - 1)))(ps, k, cc, fg, x1b),
                                        waits=w, incs=([(ct, 1)] if k == NF - 1 else []))
                            vq = arena[:, r * D + mh * 1024 + fg * 512:
                                       r * D + mh * 1024 + (fg + 1) * 512]
                            ew = entry if ci == 0 else [prev_mark]
                            if evac == "v":
                                pr.emit("scalar", (lambda vq, ps: lambda e: e.copy(vq, ps))(vq, ps),
                                        waits=[ct] + ew, incs=[(ca, 1)])
                                evac_marks.append((ca, ca.n))
                            elif evac == "y":
                                pr.emit("vector", (lambda vq, ps: lambda e: e.tensor_mul(vq, ps, vq))(vq, ps),
                                        waits=[ct] + ew, incs=[(cv, 1)])
                                evac_marks.append((cv, cv.n))
                            else:
                                sw = [ct] + ew + ([sil_mark[0]] if sil_mark[0] else [])
                                pr.emit("scalar", (lambda ps: lambda e: e.activation(
                                    lnscr[:, 0:512], ps, AF.Silu))(ps),
                                        waits=sw, incs=[(ca, 1)])
                                evac_marks.append((ca, ca.n))
                                pr.emit("vector", (lambda vq: lambda e: e.tensor_mul(
                                    vq, vq, lnscr[:, 0:512]))(vq),
                                        waits=[ca] + ew, incs=[(cv, 1)])
                                sil_mark[0] = (cv, cv.n)
                            ci += 1
                            if pump is not None:
                                pump(pn)
            if evac == "g":
                return sil_mark[0]
            return evac_marks[-1]

        a1_mark = proj_v_tm(wv_h, p0_marks[-1], pump1, 1)

        def dbg_dump(nm, _unused=None):
            if not KDBG:
                return
            w = snap()
            DBGM.clear()
            for rr in range(NT):
                pr.emit("sync", (lambda rr: lambda e: e.dma_start(
                    dbg_h[nm][rr * 128:(rr + 1) * 128, :],
                    arena[:, rr * D:(rr + 1) * D]))(rr),
                        waits=(w if rr == 0 else []), incs=[(dDB, 16)])
            DBGM.append((dDB, dDB.n))

        dbg_dump("dbgv")

        a1done = [a1_mark]
        pr.emit("vector", lambda e: e.tensor_scalar(lnl128[:, :], lnl[:, :], 128.0, None, ALU.mult),
                waits=a1done, incs=[(cv, 1)])
        tq4 = sqdump[:, 0:1024]
        l128t = pto[:, 0:1024].bitcast(F32)
        sf32q = poolE[:, 0:2048].bitcast(F32)
        cbuf4 = poolE[:, 2048:3072].bitcast(F32)
        scan_cv = None
        gen_ca = None
        prev2 = [None, None]
        for hq in range(4):
            for hl in range(4):
                h = hq * 4 + hl
                gw = ([ca] if hq == 0 and hl == 0 else []) + list(DBGM) + \
                     ([prev2[0], prev2[1]] if (hl == 0 and hq > 0 and prev2[0]) else [])
                pr.emit("vector", (lambda h: lambda e: e.tensor_scalar(
                    tgf[:, :], emat[:, :], lnl[:, h:h + 1], 0.0, ALU.mult, ALU.min))(h),
                        waits=gw + (a1done if h == 0 else []) + ([gen_ca] if gen_ca else []))
                pr.emit("scalar", (lambda hl: lambda e: e.activation(
                    tq4[:, hl * 256:hl * 256 + 128], tgf[:, :], AF.Exp))(hl),
                        waits=[cv], incs=[(ca, 1)])
                pr.emit("vector", (lambda hl: lambda e: e.tensor_mul(
                    tq4[:, hl * 256:hl * 256 + 128], tq4[:, hl * 256:hl * 256 + 128],
                    mlo[:, :]))(hl), waits=[ca])
                pr.emit("vector", (lambda h: lambda e: e.tensor_scalar(
                    tgf[:, :], emat[:, :], lnl[:, h:h + 1], lnl128[:, h:h + 1],
                    ALU.mult, ALU.add))(h))
                pr.emit("scalar", (lambda hl: lambda e: e.activation(
                    tq4[:, hl * 256 + 128:hl * 256 + 256], tgf[:, :], AF.Exp))(hl),
                        waits=[cv], incs=[(ca, 1)])
                gen_ca = (ca, ca.n)
                pr.emit("vector", (lambda hl: lambda e: e.tensor_mul(
                    tq4[:, hl * 256 + 128:hl * 256 + 256],
                    tq4[:, hl * 256 + 128:hl * 256 + 256], mhi[:, :]))(hl),
                        waits=[ca])
                pr.emit("vector", (lambda h, hl: lambda e: e.tensor_copy(
                    l128t[:, hl * 128:(hl + 1) * 128],
                    l128[:, h:h + 1].broadcast_to([128, 128])))(h, hl))
            pr.emit("vector", lambda e: e.memset(cbuf4[:, :], 0.0), incs=[(cv, 1)])
            tgen_mark = (cv, cv.n)
            for r in range(NT):
                psx = psA if r % 2 == 0 else psB
                sf = sf32q[:, (r % 2) * 512:(r % 2) * 512 + 512]
                w0 = [tgen_mark]
                if prev2[r % 2] is not None:
                    w0.append(prev2[r % 2])
                for hl in range(4):
                    h = hq * 4 + hl
                    vsl = arena[:, r * D + h * 128: r * D + h * 128 + 128]
                    pr.emit("tensor", (lambda psx, vsl, hl: lambda e: e.matmul(
                        psx[:, 1024 + hl * 256: 1024 + hl * 256 + 128],
                        tq4[:, hl * 256:hl * 256 + 128], vsl,
                        start=True, stop=True))(psx, vsl, hl),
                            waits=(w0 if hl == 0 else []))
                    pr.emit("tensor", (lambda psx, vsl, hl: lambda e: e.matmul(
                        psx[:, 1024 + hl * 256 + 128: 1024 + hl * 256 + 256],
                        tq4[:, hl * 256 + 128:hl * 256 + 256], vsl,
                        start=True, stop=True))(psx, vsl, hl),
                            incs=([(ct, 1)] if hl == 3 else []))
                mmm = (ct, ct.n)
                pv = psx[:, 1024:2048].rearrange("p (hl pq f) -> p hl pq f", hl=4, pq=2)
                pP = pv[:, :, 0, :]
                pQ = pv[:, :, 1, :]
                sf3 = sf.rearrange("p (hl f) -> p hl f", hl=4)
                pr.emit("vector", (lambda sf3, pP: lambda e: e.tensor_add(
                    sf3, pP, cbuf4[:, :].rearrange("p (hl f) -> p hl f", hl=4)))(sf3, pP),
                        waits=[mmm])
                pr.emit("vector", (lambda sf: lambda e: e.tensor_mul(
                    cbuf4[:, :], sf, l128t[:, :]))(sf))
                pr.emit("vector", (lambda pQ: lambda e: e.tensor_add(
                    cbuf4[:, :].rearrange("p (hl f) -> p hl f", hl=4),
                    cbuf4[:, :].rearrange("p (hl f) -> p hl f", hl=4), pQ))(pQ))
                pr.emit("vector", (lambda r, hq, sf3: lambda e: e.tensor_copy(
                    arena[:, r * D + hq * 512: r * D + (hq + 1) * 512].rearrange(
                        "p (hl f) -> p hl f", hl=4), sf3))(r, hq, sf3),
                        incs=[(cv, 1)])
                prev2[r % 2] = (cv, cv.n)
        scan_mark = (cv, cv.n)
        scan_mark = (cv, cv.n)

        dbg_dump("dbg2")

        a3_mark = proj_v_tm(wq_h, scan_mark, pump1, 2, evac="y")
        dbg_dump("dbg3")

        exhaust1()

        a3done = [a3_mark] + list(DBGM)
        pr.emit("sync", lambda e: e.dma_start(lngr[:, :], lngr_h),
                waits=a3done + [(ct, ct.n)], incs=[(dSM, 16)])
        pr.emit("sync", lambda e: e.dma_start(lnbr[:, :], lnbr_h), incs=[(dSM, 16)])
        gload = (dSM, dSM.n)
        for n in range(4):
            pr.emit("tensor", (lambda n: lambda e: e.matmul(
                psA[:, n * 512:(n + 1) * 512], ones1[:, :],
                lngr[0:1, n * 512:(n + 1) * 512], start=True, stop=True))(n),
                    waits=([gload] + a3done if n == 0 else []))
            pr.emit("tensor", (lambda n: lambda e: e.matmul(
                psB[:, n * 512:(n + 1) * 512], ones1[:, :],
                lnbr[0:1, n * 512:(n + 1) * 512], start=True, stop=True))(n),
                    incs=([(ct, 1)] if n == 3 else []))
        gbcast = (ct, ct.n)
        a4marks = [None, None]
        for r in range(NT):
            ar = arena[:, r * D:(r + 1) * D]
            rs = 0 if r % 2 == 0 else 6
            w0 = (a3done if r < 2 else [a4marks[r % 2]])
            pr.emit("scalar", (lambda ar, rs: lambda e: e.activation(
                sqdump[:, :], ar, AF.Square, accum_out=srow[:, rs:rs + 1]))(ar, rs),
                    waits=w0 + [cv], incs=[(ca, 1)])
            pr.emit("vector", (lambda ar, rs: lambda e: e.reduce_sum(
                srow[:, rs + 1:rs + 2], ar, axis=AX.X))(ar, rs), waits=w0)
            pr.emit("vector", (lambda rs: lambda e: e.tensor_scalar(
                srow[:, rs + 2:rs + 3], srow[:, rs + 1:rs + 2], 1.0 / D, None, ALU.mult))(rs))
            pr.emit("vector", (lambda rs: lambda e: e.tensor_mul(
                srow[:, rs + 3:rs + 4], srow[:, rs + 2:rs + 3], srow[:, rs + 2:rs + 3]))(rs))
            pr.emit("vector", (lambda rs: lambda e: e.scalar_tensor_tensor(
                srow[:, rs + 4:rs + 5], srow[:, rs:rs + 1], 1.0 / D,
                srow[:, rs + 3:rs + 4], ALU.mult, ALU.subtract))(rs),
                    waits=[ca], incs=[(cv, 1)])
            pr.emit("scalar", (lambda rs: lambda e: e.activation(
                srow[:, rs + 5:rs + 6], srow[:, rs + 4:rs + 5], AF.Ln, bias=epsb[:, :]))(rs),
                    waits=[cv])
            pr.emit("scalar", (lambda rs: lambda e: e.activation(
                srow[:, rs + 5:rs + 6], srow[:, rs + 5:rs + 6], AF.Exp, scale=mhalf[:, :]))(rs),
                    incs=[(ca, 1)])
            pr.emit("vector", (lambda ar, rs: lambda e: e.tensor_scalar(
                lnscr[:, :], ar, srow[:, rs + 2:rs + 3], srow[:, rs + 5:rs + 6],
                ALU.subtract, ALU.mult))(ar, rs), waits=[ca])
            pr.emit("vector", lambda e: e.tensor_mul(
                lnscr2[:, :], lnscr[:, :], psA[:, :]), waits=[gbcast])
            pr.emit("vector", (lambda ar: lambda e: e.tensor_add(
                ar, lnscr2[:, :], psB[:, :]))(ar),
                    incs=[(cv, 1)])
            a4marks[r % 2] = (cv, cv.n)
            pump2(2)
        a4_mark = (cv, cv.n)

        dbg_dump("dbg4")

        a5_mark = proj_v_tm(wg_h, a4_mark, pump2, 1, evac="g")
        dbg_dump("dbg5")

        a5done = [a5_mark, (ct, ct.n)]
        for r in range(NT):
            pr.emit("sync", (lambda r: lambda e: e.dma_start(
                x1tm_h[r * 128:(r + 1) * 128, :], arena[:, r * D:(r + 1) * D]))(r),
                    waits=(a5done if r == 0 else []), incs=[(dYW, 16)])
        yw = (dYW, dYW.n)
        for h in range(H):
            pr.emit("sync", (lambda h: lambda e: e.dma_start_transpose(
                arena[:, h * D:(h + 1) * D],
                x1tm_h[0:TOK, h * 128:(h + 1) * 128]))(h),
                    waits=([yw] + list(DBGM) if h == 0 else []), incs=[(dYR, 16)])
        yr = (dYR, dYR.n)
        ev_mark = None
        for mh in range(2):
            load_attn_w_half(wo_h, mh, (a5done + [yr]) if mh == 0 else [(ct, ct.n)])
            wload = (dG, dG.n)
            for ml in range(8):
                m = mh * 8 + ml
                ps = psA if m % 2 == 0 else psB
                for h in range(H):
                    for n in range(4):
                        w = []
                        if h == 0 and n == 0:
                            w = [wload, yr]
                            if ev_mark is not None:
                                w.append(ev_mark)
                        pr.emit("tensor", (lambda ps, ml, h, n: lambda e: e.matmul(
                            ps[:, n * 512:(n + 1) * 512],
                            wsb[:, h * 1024 + ml * 128: h * 1024 + ml * 128 + 128],
                            arena[:, h * D + n * 512: h * D + (n + 1) * 512],
                            start=(h == 0), stop=(h == H - 1)))(ps, ml, h, n),
                                waits=w, incs=([(ct, 1)] if h == H - 1 and n == 3 else []))
                st = a6stage[:, (m % 2) * D:(m % 2 + 1) * D]
                pr.emit("scalar", (lambda st, ps: lambda e: e.copy(st, ps[:, :]))(st, ps),
                        waits=[ct, (dAW, dAW.n)], incs=[(ca, 1)])
                ev_mark = (ca, ca.n)
                pr.emit("sync", (lambda st, m: lambda e: e.dma_start(
                    a_h[m * 128:(m + 1) * 128, :], st))(st, m),
                        waits=[ca], incs=[(dAW, 16)])
                pump2(5)

        a6_ct_saved = ct.n
        a6done = [(dAW, dAW.n), (ct, ct.n), (ca, ca.n)]
        bat = [poolA[:, 0:D], poolA[:, D:2 * D]]
        bxs = [poolA[:, 2 * D:6 * D].bitcast(F32)[:, 0:D],
               poolA[:, 2 * D:6 * D].bitcast(F32)[:, D:2 * D]]
        br1 = [poolA[:, 6 * D:10 * D].bitcast(F32)[:, 0:D],
               poolA[:, 6 * D:10 * D].bitcast(F32)[:, D:2 * D]]
        bx2 = [poolA[:, 10 * D:12 * D][:, 0:D], poolA[:, 10 * D:12 * D][:, D:2 * D]]
        for r in range(NT):
            b = r % 2
            at = bat[b]
            xs = bxs[b]
            r1 = br1[b]
            x2 = bx2[b]
            dL = dL0 if b == 0 else dL1
            w0 = a6done if r == 0 else ([cv, ca] if r >= 2 else [])
            pr.emit("sync", (lambda at, r: lambda e: e.dma_start_transpose(
                at, a_h[0:D, r * 128:(r + 1) * 128]))(at, r), waits=w0, incs=[(dL, 16)])
            pr.emit("sync", (lambda xs, r: lambda e: e.dma_start(
                xs, x_h[r * 128:(r + 1) * 128, :]))(xs, r), incs=[(dL, 16)])
            rs = 0 if b == 0 else 6
            ln_tm_chunk(at, r1, rs, dR1, extra_waits=[dL])
            pr.emit("vector", (lambda r1, xs: lambda e: e.tensor_add(r1, r1, xs))(r1, xs),
                    incs=[(cv, 1)])
            pr.emit("sync", (lambda r1, r: lambda e: e.dma_start(
                r1tm_h[r * 128:(r + 1) * 128, :], r1))(r1, r), waits=[cv], incs=[(dR1, 16)])
            ln_tm_chunk(r1, x2, rs, dX2)
            pr.emit("sync", (lambda x2, r: lambda e: e.dma_start(
                x2tm_h[r * 128:(r + 1) * 128, :], x2))(x2, r), waits=[cv], incs=[(dX2, 16)])
            if r == 7:
                dX2_half0 = dX2.n
            pump2(1)

        exhaust2()
        fin_marks = []
        a6_ct = a6_ct_saved
        a6b_done = [(dR1, dR1.n), (dX2, dX2.n), (cv, cv.n), (ca, ca.n)]

        for half in range(2):
            t0 = half * HT
            if half == 0:
                hw = [(dX2, dX2_half0), a5_mark, (ct, a6_ct)] + list(DBGM)
            else:
                hw = [(dX2, dX2.n), (ct, ct.n)] + list(DBGM)
            for f in range(NF):
                pr.emit("sync", (lambda f, t0: lambda e: e.dma_start_transpose(
                    x2h[:, f * HT:(f + 1) * HT],
                    x2tm_h[t0:t0 + HT, f * 128:(f + 1) * 128]))(f, t0),
                        waits=(hw + [dX2] if f == 0 else []), incs=[(dXH, 16)])
            x2load = (dXH, dXH.n)
            S = [poolB, poolC]
            dUs = [dU, dGS]

            def slab_load(buf, sem, hsrc, j, waits):
                kk = NF if hsrc is not td_h else KG // 128
                cols = hsrc[:, j * KG:(j + 1) * KG] if hsrc is not td_h else hsrc[j * KG:(j + 1) * KG, :]
                pr.emit("sync", (lambda buf, cols, kk: lambda e: e.dma_start(
                    buf[:, 0:kk * (KG if kk == NF else D)].rearrange("p (k n) -> p k n", k=kk),
                    cols.rearrange("(k p) n -> p k n", p=128)))(buf, cols, kk),
                        waits=waits, incs=[(sem, 16)])

            if half == 0:
                slab_load(S[0], dUs[0], tu_h, 0, [(dA, dA.n), (cp, cp.n), (cv, cv.n)])
                slab_load(S[1], dUs[1], tg_h, 0, [])
            else:
                slab_load(S[0], dUs[0], tu_h, 0, [(ct, ct.n)])
                slab_load(S[1], dUs[1], tg_h, 0, [])
            hb_marks = []
            dn_marks = []
            for j in range(NKG):
                ub, gb = S[j % 2], S[(j + 1) % 2]
                usem, gsem = dUs[j % 2], dUs[(j + 1) % 2]
                uload = (usem, usem.n)
                gload = (gsem, gsem.n)
                up_marks = []
                for me in range(KG // 128):
                    pme = psA if me % 2 == 0 else psB
                    gi = len(hb_marks)
                    for k in range(NF):
                        w = []
                        if k == 0:
                            w = [uload, x2load]
                            if me >= 2:
                                w.append(up_marks[me - 2])
                            elif gi >= 2:
                                w += [hb_marks[gi - 2][0], hb_marks[gi - 2][1]]
                            if me == 0 and half > 0 and j == 0:
                                w += [(cv, cv.n), (ca, ca.n)]
                            if me == 0 and dn_marks:
                                w.append(dn_marks[-1])
                        for n in range(2):
                            pr.emit("tensor", (lambda pme, ub, me, k, n: lambda e: e.matmul(
                                pme[:, n * 512:n * 512 + 512],
                                ub[:, k * KG + me * 128: k * KG + (me + 1) * 128],
                                x2h[:, k * HT + n * 512: k * HT + (n + 1) * 512],
                                start=(k == 0), stop=(k == NF - 1)))(pme, ub, me, k, n),
                                    waits=(w if n == 0 else []),
                                    incs=([(ct, 1)] if k == NF - 1 and n == 1 else []))
                    pr.emit("scalar", (lambda pme, me: lambda e: e.activation(
                        hbuf[:, me * HT:(me + 1) * HT], pme[:, 0:HT],
                        AF.Silu, scale=su2b[:, :]))(pme, me),
                            waits=[ct], incs=[(ca, 1)])
                    up_marks.append((ca, ca.n))
                slab_load(ub, usem, td_h, j, [(ct, ct.n)])
                dload = (usem, usem.n)
                for me in range(KG // 128):
                    pme = psA if me % 2 == 0 else psB
                    gi = len(hb_marks)
                    for k in range(NF):
                        w = []
                        if k == 0:
                            w = [gload, x2load]
                            if gi >= 2:
                                w.append(hb_marks[gi - 2][1])
                        for n in range(2):
                            pr.emit("tensor", (lambda pme, gb, me, k, n: lambda e: e.matmul(
                                pme[:, HT + n * 512: HT + n * 512 + 512],
                                gb[:, k * KG + me * 128: k * KG + (me + 1) * 128],
                                x2h[:, k * HT + n * 512: k * HT + (n + 1) * 512],
                                start=(k == 0), stop=(k == NF - 1)))(pme, gb, me, k, n),
                                    waits=(w if n == 0 else []),
                                    incs=([(ct, 1)] if k == NF - 1 and n == 1 else []))
                    pr.emit("vector", (lambda pme, me: lambda e: e.tensor_mul(
                        hbuf[:, me * HT:(me + 1) * HT], hbuf[:, me * HT:(me + 1) * HT],
                        pme[:, HT:2 * HT]))(pme, me),
                            waits=[ct, up_marks[me]], incs=[(cv, 1)])
                    hb_marks.append(((ca, ca.n), (cv, cv.n)))
                if j + 1 < NKG:
                    slab_load(gb, gsem, tu_h, j + 1, [(ct, ct.n)])
                hlast = hb_marks[-1]
                for md in range(NF):
                    ps = (psA if md % 2 == 0 else psB)[:, 0:HT]
                    di = len(dn_marks)
                    for k in range(KG // 128):
                        w = []
                        if k == 0:
                            w = [dload, hlast[1]]
                            if di >= 2:
                                w.append(dn_marks[di - 2])
                        for n in range(2):
                            pr.emit("tensor", (lambda ps, ub, md, k, n: lambda e: e.matmul(
                                ps[:, n * 512:(n + 1) * 512],
                                ub[:, k * D + md * 128: k * D + (md + 1) * 128],
                                hbuf[:, k * HT + n * 512: k * HT + (n + 1) * 512],
                                start=(k == 0), stop=(k == KG // 128 - 1)))(ps, ub, md, k, n),
                                    waits=(w if n == 0 else []),
                                    incs=([(ct, 1)] if (k == KG // 128 - 1 and n == 1) else []))
                    acs = accf[:, md * HT:(md + 1) * HT]
                    if j == 0:
                        aw = [ct] + (fin_marks if (half == 1 and md == 0) else []) + \
                             (a6b_done if (half == 0 and md == 0) else [])
                        pr.emit("vector", (lambda acs, ps: lambda e: e.tensor_copy(acs, ps))(acs, ps),
                                waits=aw, incs=[(cv, 1)])
                    else:
                        pr.emit("vector", (lambda acs, ps: lambda e: e.tensor_add(acs, acs, ps))(acs, ps),
                                waits=[ct], incs=[(cv, 1)])
                    dn_marks.append((cv, cv.n))
                if j + 1 < NKG:
                    slab_load(ub, usem, tg_h, j + 1, [(ct, ct.n)])
            ffdone = snap()
            for m in range(NF):
                st = dstage[:, m * HT:(m + 1) * HT]
                pr.emit("scalar", (lambda st, m: lambda e: e.copy(
                    st, accf[:, m * HT:(m + 1) * HT]))(st, m),
                        waits=(ffdone if m == 0 else []), incs=[(ca, 1)])
                pr.emit("sync", (lambda st, m, t0: lambda e: e.dma_start(
                    d_scr_h[m * 128:(m + 1) * 128, t0:t0 + HT], st))(st, m, t0),
                        waits=[ca], incs=[(dDW, 16)])
            cast_mark = (ca, ca.n)
            fdT = [poolA[:, 0:D], poolA[:, D:2 * D]]
            fr1 = [poolA[:, 2 * D:6 * D].bitcast(F32)[:, 0:D],
                   poolA[:, 2 * D:6 * D].bitcast(F32)[:, D:2 * D]]
            foc = [poolA[:, 6 * D:10 * D].bitcast(F32)[:, 0:D],
                   poolA[:, 6 * D:10 * D].bitcast(F32)[:, D:2 * D]]
            for c in range(NT // 2):
                r = half * (NT // 2) + c
                b = c % 2
                dt_ = fdT[b]
                r1 = fr1[b]
                oc = foc[b]
                w0 = [(dDW, dDW.n), (dR1, dR1.n), cast_mark, (ct, ct.n)] if c == 0 else                     ([cv, ca] if c >= 2 else [])
                pr.emit("sync", (lambda dt_, r: lambda e: e.dma_start_transpose(
                    dt_, d_scr_h[0:D, r * 128:(r + 1) * 128]))(dt_, r),
                        waits=w0, incs=[(dL0 if b == 0 else dL1, 16)])
                pr.emit("sync", (lambda r1, r: lambda e: e.dma_start(
                    r1, r1tm_h[r * 128:(r + 1) * 128, :]))(r1, r),
                        incs=[(dL0 if b == 0 else dL1, 16)])
                rs = 0 if b == 0 else 6
                ln_tm_chunk(dt_, oc, rs, dOU, extra_waits=[dL0 if b == 0 else dL1])
                pr.emit("vector", (lambda oc, r1: lambda e: e.tensor_add(oc, oc, r1))(oc, r1),
                        incs=[(cv, 1)])
                pr.emit("sync", (lambda oc, r: lambda e: e.dma_start(
                    out_h[r * 128:(r + 1) * 128, :], oc))(oc, r), waits=[cv], incs=[(dOU, 16)])
            fin_marks = [(dOU, dOU.n), (cv, cv.n), (ca, ca.n)]

        fins = snap()
        pr.emit("gpsimd", lambda e: e.nop(), waits=fins)

        @block.sync
        def _(eng):
            pr.run_engine("sync", eng)

        @block.gpsimd
        def _(eng):
            pr.run_engine("gpsimd", eng)

        @block.vector
        def _(eng):
            pr.run_engine("vector", eng)

        @block.scalar
        def _(eng):
            pr.run_engine("scalar", eng)

        @block.tensor
        def _(eng):
            pr.run_engine("tensor", eng)

    return nc


def make_core_inputs(L0_attn_x, L0_attn_Wq, L0_attn_Wv, L0_attn_Wg, L0_attn_Wo,
                     L0_attn_beta, L0_attn_ln_g, L0_attn_ln_b,
                     L0_ffn_W_up, L0_ffn_W_gate, L0_ffn_W_down):
    common = {
        "wq": np.ascontiguousarray(L0_attn_Wq, dtype=np.float32),
        "wv": np.ascontiguousarray(L0_attn_Wv, dtype=np.float32),
        "wg": np.ascontiguousarray(L0_attn_Wg, dtype=np.float32),
        "wo": np.ascontiguousarray(L0_attn_Wo, dtype=np.float32),
        "beta": np.ascontiguousarray(np.asarray(L0_attn_beta, dtype=np.float32).reshape(1, H)),
        "lngr": np.ascontiguousarray(np.asarray(L0_attn_ln_g, dtype=np.float32).reshape(1, D)),
        "lnbr": np.ascontiguousarray(np.asarray(L0_attn_ln_b, dtype=np.float32).reshape(1, D)),
        "wuT": np.ascontiguousarray(np.asarray(L0_ffn_W_up, dtype=np.float32).T),
        "wgT": np.ascontiguousarray(np.asarray(L0_ffn_W_gate, dtype=np.float32).T),
        "wdT": np.ascontiguousarray(np.asarray(L0_ffn_W_down, dtype=np.float32).T),
        "emat": np.ascontiguousarray(
            np.arange(128, dtype=np.float32)[None, :] - np.arange(128, dtype=np.float32)[:, None]),
        "mlo": np.ascontiguousarray(
            (np.arange(128)[None, :] >= np.arange(128)[:, None]).astype(ml_dtypes.bfloat16)),
        "mhi": np.ascontiguousarray(
            (np.arange(128)[None, :] < np.arange(128)[:, None]).astype(ml_dtypes.bfloat16)),
    }
    xs = np.ascontiguousarray(np.asarray(L0_attn_x, dtype=np.float32))
    in_maps = []
    for b in range(NCORES):
        m = dict(common)
        m["x"] = np.ascontiguousarray(xs[b])
        in_maps.append(m)
    return in_maps


def kernel(**inputs):
    in_maps = make_core_inputs(**inputs)
    nc = build_program()
    res = run_bass_kernel_spmd(nc, in_maps, core_ids=list(range(NCORES)))
    out = np.stack([np.asarray(r["out"]) for r in res.results], axis=0)
    return out.astype(np.float32)


# version 18 (final)
# speedup vs baseline: 1.0357x; 1.0357x over previous
import os
import numpy as np
import ml_dtypes
import concourse.bass as bass
import concourse.mybir as mybir
from concourse.bass_utils import run_bass_kernel_spmd
from contextlib import ExitStack

F32 = mybir.dt.float32
BF16 = mybir.dt.bfloat16
AF = mybir.ActivationFunctionType
ALU = mybir.AluOpType
AX = mybir.AxisListType

TOK = 2048
D = 2048
E = 8192
H = 16
NCORES = 8
NT = TOK // 128
NF = D // 128
NQ = 4
QT = TOK // NQ
HT = TOK // 2
KG = 1024
NKG = E // KG
LN_EPS = 1e-5


class Sem:
    def __init__(self, es, nc, name, owner):
        self.h = es.enter_context(nc.semaphore(name))
        self.owner = owner
        self.n = 0


class Prog:

    def __init__(self):
        self.events = []
        self.own = {}

    def emit(self, engine, fn, waits=(), incs=()):
        incs = list(incs)
        is_dma = any(k >= 16 for (_, k) in incs)
        auto = self.own.get(engine) if engine in ("vector", "scalar", "gpsimd") else None
        w = []
        if auto is not None and auto.n > 0:
            w.append((auto.h, auto.n))
        for s in waits:
            if isinstance(s, tuple):
                s, v = s
            else:
                v = s.n
            if v > 0:
                w.append((s.h, v))
        if auto is not None and not is_dma and not any(s is auto for (s, _) in incs):
            incs.append((auto, 1))
        for (s, k) in incs:
            assert s.owner == engine, f"sem owned by {s.owner}, inc from {engine}"
            s.n += k
        self.events.append((engine, fn, w, [(s.h, k) for (s, k) in incs]))

    def run_engine(self, engine_name, eng):
        for (en, fn, waits, incs) in self.events:
            if en != engine_name:
                continue
            for (sh, v) in waits:
                eng.wait_ge(sh, v)
            inst = fn(eng)
            for (sh, k) in incs:
                inst.then_inc(sh, k)


def build_program():
    nc = bass.Bass("TRN2", target_bir_lowering=False)

    x_h = nc.dram_tensor("x", [TOK, D], F32, kind="ExternalInput").ap()
    wq_h = nc.dram_tensor("wq", [D, D], F32, kind="ExternalInput").ap()
    wv_h = nc.dram_tensor("wv", [D, D], F32, kind="ExternalInput").ap()
    wg_h = nc.dram_tensor("wg", [D, D], F32, kind="ExternalInput").ap()
    wo_h = nc.dram_tensor("wo", [D, D], F32, kind="ExternalInput").ap()
    beta_h = nc.dram_tensor("beta", [1, H], F32, kind="ExternalInput").ap()
    lngr_h = nc.dram_tensor("lngr", [1, D], F32, kind="ExternalInput").ap()
    lnbr_h = nc.dram_tensor("lnbr", [1, D], F32, kind="ExternalInput").ap()
    wuT_h = nc.dram_tensor("wuT", [D, E], F32, kind="ExternalInput").ap()
    wgT_h = nc.dram_tensor("wgT", [D, E], F32, kind="ExternalInput").ap()
    wdT_h = nc.dram_tensor("wdT", [E, D], F32, kind="ExternalInput").ap()
    emat_h = nc.dram_tensor("emat", [128, 128], F32, kind="ExternalInput").ap()
    mlo_h = nc.dram_tensor("mlo", [128, 128], BF16, kind="ExternalInput").ap()
    mhi_h = nc.dram_tensor("mhi", [128, 128], BF16, kind="ExternalInput").ap()
    out_h = nc.dram_tensor("out", [TOK, D], F32, kind="ExternalOutput").ap()

    x1tm_h = nc.dram_tensor("x1tm", [TOK, D], BF16, kind="Internal").ap()
    a_h = nc.dram_tensor("a_scr", [D, TOK], BF16, kind="Internal").ap()
    r1tm_h = nc.dram_tensor("r1tm", [TOK, D], F32, kind="Internal").ap()
    x2tm_h = nc.dram_tensor("x2tm", [TOK, D], BF16, kind="Internal").ap()
    tu_h = nc.dram_tensor("tu", [D, E], BF16, kind="Internal").ap()
    tg_h = nc.dram_tensor("tg", [D, E], BF16, kind="Internal").ap()
    td_h = nc.dram_tensor("td", [E, D], BF16, kind="Internal").ap()
    d_scr_h = nc.dram_tensor("d_scr", [D, TOK], BF16, kind="Internal").ap()
    KDBG = os.environ.get("KDBG", "0") == "1"
    if KDBG:
        dbg_h = {nm: nc.dram_tensor(nm, [D, TOK], BF16, kind="Internal").ap()
                 for nm in ("dbgv", "dbg2", "dbg3", "dbg4", "dbg5")}

    with ExitStack() as es:
        sbuf = lambda name, n_bf16: es.enter_context(nc.sbuf_tensor(name, [128, n_bf16], BF16))
        poolA = sbuf("poolA", 17 * TOK)
        poolB = sbuf("poolB", 8 * D)
        poolC = sbuf("poolC", 8 * D)
        poolE = sbuf("poolE", 12 * D)
        sqdump = sbuf("sqdump", D)

        arena = poolA
        accf = poolA[:, 0:2 * NF * HT].bitcast(F32)
        silbuf = poolA[:, 2 * NF * HT:2 * NF * HT + HT]
        wsb = poolB
        uslab = poolB
        dstage = poolB
        pin = poolC[:, 0:4 * D].bitcast(F32)
        pc1 = poolC[:, 4 * D:5 * D]
        pc2 = poolC[:, 5 * D:6 * D]
        pto = poolC[:, 6 * D:8 * D]
        gslab = poolC
        a6stage = poolE[:, 0:2 * D]
        x1q = poolE[:, 0:NF * QT]
        hbuf = poolE[:, 0:(KG // 128) * HT]
        x2h = poolE[:, 4 * D:4 * D + NF * HT]
        tma = poolE[:, 0:2 * D]
        tmx = poolE[:, 2 * D:6 * D].bitcast(F32)
        tmo = poolE[:, 6 * D:10 * D].bitcast(F32)
        tms = poolE[:, 10 * D:12 * D]
        yrow = poolE[0:1, 0:8 * D].bitcast(F32)
        lnscr = poolE[:, 8 * D:9 * D]
        lnscr2 = poolE[:, 9 * D:10 * D]

        sb2 = lambda name, shape, dt: es.enter_context(nc.sbuf_tensor(name, shape, dt))
        lamb = sb2("lamb", [128, H], F32)
        lngr = poolE[0:1, 0:2 * D].bitcast(F32)
        lnbr = poolE[0:1, 2 * D:4 * D].bitcast(F32)
        ones1 = sb2("ones1", [1, 128], F32)
        ones128 = sb2("ones128", [128, 1], BF16)
        ones128f = sb2("ones128f", [128, 1], F32)
        epsb = sb2("epsb", [128, 1], F32)
        mhalf = sb2("mhalf", [128, 1], F32)
        c128b = sb2("c128b", [128, 1], F32)
        taub = sb2("taub", [128, 3], F32)
        ntaub = sb2("ntaub", [128, 3], F32)
        su2b = sb2("su2b", [128, 1], F32)
        psums = sb2("psums", [128, 3 * 64], F32)
        pdump = sbuf("pdump", D)
        srow = sb2("srow", [128, 12], F32)
        rowbuf = sb2("rowbuf", [1, 128], F32)
        brow = sb2("brow", [1, H], F32)
        emat = sb2("emat_s", [128, 128], F32)
        mlo = sb2("mlo_s", [128, 128], BF16)
        mhi = sb2("mhi_s", [128, 128], BF16)
        tgen = sb2("tgen", [128, 256], BF16)
        tgf = sb2("tgf", [128, 128], F32)
        sf32 = sb2("sf32", [128, 256], F32)
        cbuf = sb2("cbuf", [128, 128], F32)
        lnl = sb2("lnl", [128, H], F32)
        l128 = sb2("l128", [128, H], F32)
        lnl128 = sb2("lnl128", [128, H], F32)
        psA = es.enter_context(nc.psum_tensor("psA", [128, D], F32))
        psB = es.enter_context(nc.psum_tensor("psB", [128, D], F32))

        mks = lambda nm, ow: Sem(es, nc, nm, ow)
        dSM = mks("dSM", "sync")
        dL0 = mks("dL0", "sync")
        dL1 = mks("dL1", "sync")
        dX1 = mks("dX1", "sync")
        dXQ = mks("dXQ", "sync")
        dXQb = mks("dXQb", "sync")
        dAW = mks("dAW", "sync")
        dR1 = mks("dR1", "sync")
        dX2 = mks("dX2", "sync")
        dXH = mks("dXH", "sync")
        dDW = mks("dDW", "sync")
        dOU = mks("dOU", "sync")
        dU = mks("dU", "sync")
        dGS = mks("dGS", "sync")
        dP0 = mks("dP0", "gpsimd")
        dP1 = mks("dP1", "gpsimd")
        dG = mks("dG", "gpsimd")
        dA = mks("dA", "scalar")
        dDB = mks("dDB", "sync")
        dYW = mks("dYW", "sync")
        dYR = mks("dYR", "sync")
        ct = mks("ct", "tensor")
        cv = mks("cv", "vector")
        ca = mks("ca", "scalar")
        cp = mks("cp", "gpsimd")
        ALL = [dSM, dL0, dL1, dX1, dXQ, dXQb, dAW, dR1, dX2, dXH, dDW, dOU,
               dU, dGS, dP0, dP1, dG, dA, dDB, dYW, dYR, ct, cv, ca, cp]

        block = es.enter_context(nc.Block())
        pr = Prog()
        pr.own = {"vector": cv, "scalar": ca, "gpsimd": cp}

        def snap():
            return [(s, s.n) for s in ALL]

        pr.emit("vector", lambda e: e.memset(ones1[:, :], 1.0), incs=[(cv, 1)])
        pr.emit("vector", lambda e: e.memset(ones128[:, :], 1.0), incs=[(cv, 1)])
        pr.emit("vector", lambda e: e.memset(ones128f[:, :], 1.0), incs=[(cv, 1)])
        pr.emit("vector", lambda e: e.memset(epsb[:, :], LN_EPS), incs=[(cv, 1)])
        pr.emit("vector", lambda e: e.memset(mhalf[:, :], -0.5), incs=[(cv, 1)])
        pr.emit("vector", lambda e: e.memset(c128b[:, :], 128.0), incs=[(cv, 1)])
        pr.emit("sync", lambda e: e.dma_start(brow[:, :], beta_h), incs=[(dSM, 16)])

        pr.emit("sync", lambda e: e.dma_start(emat[:, :], emat_h), incs=[(dSM, 16)])
        pr.emit("sync", lambda e: e.dma_start(mlo[:, :], mlo_h), incs=[(dSM, 16)])
        pr.emit("sync", lambda e: e.dma_start(mhi[:, :], mhi_h), incs=[(dSM, 16)])
        pr.emit("scalar", lambda e: e.activation(brow[:, :], brow[:, :], AF.Sigmoid),
                waits=[dSM], incs=[(ca, 1)])
        pr.emit("tensor", lambda e: e.matmul(psB[:, 0:H], ones1[:, :], brow[:, :],
                                             start=True, stop=True),
                waits=[ca, cv], incs=[(ct, 1)])
        pr.emit("vector", lambda e: e.tensor_copy(lamb[:, :], psB[:, 0:H]),
                waits=[ct], incs=[(cv, 1)])
        pr.emit("scalar", lambda e: e.activation(lnl[:, :], lamb[:, :], AF.Ln),
                waits=[cv], incs=[(ca, 1)])
        pr.emit("scalar", lambda e: e.activation(l128[:, :], lnl[:, :], AF.Exp, scale=c128b[:, :]),
                incs=[(ca, 1)])

        def ln_tm_chunk(src, dst, rs, wb, extra_waits=()):
            s = src
            pr.emit("scalar", (lambda s, rs: lambda e: e.activation(
                sqdump[:, :], s, AF.Square, accum_out=srow[:, rs:rs + 1]))(s, rs),
                    waits=list(extra_waits) + [cv], incs=[(ca, 1)])
            pr.emit("vector", (lambda s, rs: lambda e: e.reduce_sum(
                srow[:, rs + 1:rs + 2], s, axis=AX.X))(s, rs),
                    waits=list(extra_waits))
            pr.emit("vector", (lambda rs: lambda e: e.tensor_scalar(
                srow[:, rs + 2:rs + 3], srow[:, rs + 1:rs + 2], 1.0 / D, None, ALU.mult))(rs))
            pr.emit("vector", (lambda rs: lambda e: e.tensor_mul(
                srow[:, rs + 3:rs + 4], srow[:, rs + 2:rs + 3], srow[:, rs + 2:rs + 3]))(rs))
            pr.emit("vector", (lambda rs: lambda e: e.scalar_tensor_tensor(
                srow[:, rs + 4:rs + 5], srow[:, rs:rs + 1], 1.0 / D,
                srow[:, rs + 3:rs + 4], ALU.mult, ALU.subtract))(rs),
                    waits=[ca], incs=[(cv, 1)])
            pr.emit("scalar", (lambda rs: lambda e: e.activation(
                srow[:, rs + 5:rs + 6], srow[:, rs + 4:rs + 5], AF.Ln, bias=epsb[:, :]))(rs),
                    waits=[cv])
            pr.emit("scalar", (lambda rs: lambda e: e.activation(
                srow[:, rs + 5:rs + 6], srow[:, rs + 5:rs + 6], AF.Exp, scale=mhalf[:, :]))(rs),
                    incs=[(ca, 1)])
            pr.emit("vector", (lambda s, dst, rs: lambda e: e.tensor_scalar(
                dst, s, srow[:, rs + 2:rs + 3], srow[:, rs + 5:rs + 6],
                ALU.subtract, ALU.mult))(s, dst, rs),
                    waits=[ca, wb], incs=[(cv, 1)])

        prep_srcs = [(wuT_h, D, E), (wgT_h, D, E), (wdT_h, E, D)]
        slot_seen = set()

        def gen_p1():
            amarks = []
            ti = 0
            for mi, (wsrc, R, C) in enumerate(prep_srcs):
                nr, ncq = R // 128, C // D
                for t in range(nr * ncq):
                    rt, cq = t // ncq, t % ncq
                    b = t % 2
                    dst = pin[:, b * D:(b + 1) * D]
                    w = [amarks[ti - 2]] if ti >= 2 else []
                    pr.emit("gpsimd", (lambda wsrc, rt, cq, dst: lambda e: e.dma_start(
                        dst, wsrc[rt * 128:(rt + 1) * 128, cq * D:(cq + 1) * D]))(wsrc, rt, cq, dst),
                            waits=w, incs=[(dP0 if b == 0 else dP1, 16)])
                    slot = psums[:, mi * 64 + t: mi * 64 + t + 1]
                    dP = dP0 if b == 0 else dP1
                    pr.emit("scalar", (lambda slot, dst: lambda e: e.activation(
                        pdump[:, :], dst, AF.Abs, accum_out=slot))(slot, dst),
                            waits=[dP], incs=[(ca, 1)])
                    amarks.append((ca, ca.n))
                    ti += 1
                    yield
            for mi in range(3):
                pr.emit("vector", (lambda mi: lambda e: e.reduce_sum(
                    srow[:, 10:11], psums[:, mi * 64:mi * 64 + 64], axis=AX.X))(mi),
                        waits=[ca, cp, (ct, ct.n)], incs=[(cv, 1)])
                pr.emit("tensor", lambda e: e.matmul(psB[0:1, 1536:1537], ones128f[:, :], srow[:, 10:11],
                                                     start=True, stop=True),
                        waits=[cv], incs=[(ct, 1)])
                pr.emit("vector", (lambda mi: lambda e: e.tensor_scalar(
                    rowbuf[0:1, mi:mi + 1], psB[0:1, 1536:1537], 0.5 / (D * E), 0.5e-8,
                    ALU.mult, ALU.add))(mi), waits=[ct], incs=[(cv, 1)])
            pr.emit("tensor", lambda e: e.matmul(psB[:, 1540:1543], ones1[:, :], rowbuf[0:1, 0:3],
                                                 start=True, stop=True),
                    waits=[cv], incs=[(ct, 1)])
            pr.emit("vector", lambda e: e.tensor_copy(taub[:, :], psB[:, 1540:1543]),
                    waits=[ct], incs=[(cv, 1)])
            pr.emit("vector", lambda e: e.tensor_scalar(ntaub[:, :], taub[:, :], -1.0, None, ALU.mult),
                    incs=[(cv, 1)])
            pr.emit("vector", lambda e: e.tensor_copy(su2b[:, :], taub[:, 0:1]), incs=[(cv, 1)])
            yield

        def gen_p15():
            tern_dsts = [tu_h, tg_h, td_h]
            tmarks = []
            ti = 0
            for mi, (wsrc, R, C) in enumerate(prep_srcs):
                dst_h = tern_dsts[mi]
                nr, ncq = R // 128, C // D
                for t in range(nr * ncq):
                    rt, cq = t // ncq, t % ncq
                    b = t % 2
                    pi = pin[:, b * D:(b + 1) * D]
                    po = pto[:, b * D:(b + 1) * D]
                    tb = taub[:, mi:mi + 1]
                    ntb = ntaub[:, mi:mi + 1]
                    dP = dP0 if b == 0 else dP1
                    w = [tmarks[ti - 2][0], tmarks[ti - 2][1]] if ti >= 2 else [cv]
                    pr.emit("gpsimd", (lambda wsrc, rt, cq, pi: lambda e: e.dma_start(
                        pi, wsrc[rt * 128:(rt + 1) * 128, cq * D:(cq + 1) * D]))(wsrc, rt, cq, pi),
                            waits=w, incs=[(dP, 16)])
                    if ti % 2 == 0:
                        pr.emit("vector", (lambda pi, tb: lambda e: e.tensor_scalar(
                            pc1[:, :], pi, tb, 2.0, ALU.is_gt, ALU.mult))(pi, tb),
                                waits=[dP])
                        pr.emit("vector", (lambda pi, ntb: lambda e: e.tensor_scalar(
                            pc2[:, :], pi, ntb, -2.0, ALU.is_lt, ALU.mult))(pi, ntb))
                        pr.emit("vector", (lambda po: lambda e: e.tensor_add(po, pc1[:, :], pc2[:, :]))(po),
                                waits=[(dA, dA.n)], incs=[(cv, 1)])
                        cvm = (cv, cv.n)
                        pr.emit("scalar", (lambda dst_h, rt, cq, po: lambda e: e.dma_start(
                            dst_h[rt * 128:(rt + 1) * 128, cq * D:(cq + 1) * D], po))(dst_h, rt, cq, po),
                                waits=[cvm], incs=[(dA, 16)])
                        tmarks.append((cvm, cvm, cvm))
                    else:
                        pr.emit("scalar", (lambda pi, ntb: lambda e: e.activation(
                            pc1[:, :], pi, AF.Sign, bias=ntb))(pi, ntb),
                                waits=[dP, (cv, cv.n)])
                        pr.emit("scalar", (lambda pi, tb: lambda e: e.activation(
                            pc2[:, :], pi, AF.Sign, bias=tb))(pi, tb),
                                incs=[(ca, 1)])
                        cam = (ca, ca.n)
                        pr.emit("vector", (lambda po: lambda e: e.tensor_add(po, pc1[:, :], pc2[:, :]))(po),
                                waits=[cam, (dA, dA.n)], incs=[(cv, 1)])
                        cvm = (cv, cv.n)
                        pr.emit("scalar", (lambda dst_h, rt, cq, po: lambda e: e.dma_start(
                            dst_h[rt * 128:(rt + 1) * 128, cq * D:(cq + 1) * D], po))(dst_h, rt, cq, po),
                                waits=[cvm], incs=[(dA, 16)])
                        tmarks.append((cvm, cam, cvm))
                    ti += 1
                    yield

        _g1, _g2 = gen_p1(), gen_p15()

        def pump1(n=1):
            for _ in range(n):
                if next(_g1, "end") == "end":
                    break

        def pump2(n=1):
            for _ in range(n):
                if next(_g2, "end") == "end":
                    break

        def exhaust1():
            for _ in _g1:
                pass

        def exhaust2():
            for _ in _g2:
                pass

        p0_in = [tmx[:, 0:D], tmx[:, D:2 * D], tmo[:, 0:D], tmo[:, D:2 * D]]
        p0_out = [tms[:, 0:D], tms[:, D:2 * D], tma[:, 0:D], tma[:, D:2 * D]]
        p0_marks = []
        for r in range(NT):
            bb = r % 4
            dL = dL0 if bb % 2 == 0 else dL1
            xs = p0_in[bb]
            w = [p0_marks[r - 4]] if r >= 4 else [cv]
            pr.emit("sync", (lambda r, xs: lambda e: e.dma_start(
                xs, x_h[r * 128:(r + 1) * 128, :]))(r, xs), waits=w, incs=[(dL, 16)])
            ln_tm_chunk(xs, p0_out[bb], 0 if r % 2 == 0 else 6, dX1,
                        extra_waits=[dL])
            pr.emit("sync", (lambda r, bb: lambda e: e.dma_start(
                x1tm_h[r * 128:(r + 1) * 128, :], p0_out[bb]))(r, bb),
                    waits=[cv], incs=[(dX1, 16)])
            p0_marks.append((cv, cv.n))
            pump1(2)

        def load_attn_w_half(wh, mh, waits):
            pr.emit("gpsimd", (lambda wh, mh: lambda e: e.dma_start(
                wsb[:, :].rearrange("p (k n) -> p k n", k=NF),
                wh[:, mh * 1024:(mh + 1) * 1024].rearrange("(k p) n -> p k n", p=128)))(wh, mh),
                    waits=waits, incs=[(dG, 16)])

        x1q_bufs = [poolE[:, 0:NF * QT], poolE[:, NF * QT:2 * NF * QT]]

        def load_x1q(buf, bi, tq, waits):
            sem = dXQ if bi == 0 else dXQb
            q = "sync"
            for f in range(NF):
                pr.emit(q, (lambda buf, tq, f: lambda e: e.dma_start_transpose(
                    buf[:, f * QT:(f + 1) * QT],
                    x1tm_h[tq * QT:(tq + 1) * QT, f * 128:(f + 1) * 128]))(buf, tq, f),
                        waits=(list(waits) + [dX1] if f == 0 else []), incs=[(sem, 16)])
            return (sem, sem.n)

        def aq3(tq, m):
            return arena[:, 4 * tq * D:4 * (tq + 1) * D].rearrange(
                "p (r f) -> p r f", r=4)[:, :, m * 128:(m + 1) * 128]

        def ps3(ps):
            return ps.rearrange("p (r f) -> p r f", r=4)

        DBGM = []

        def proj_v_tm(wh, prev_mark, pump=None, pn=0, evac="v", lentry=None):
            entry = [prev_mark, (ct, ct.n)] + list(DBGM)
            if lentry is None:
                lentry = entry
            else:
                lentry = list(lentry) + list(DBGM)
            sil_mark = [None]
            ci = 0
            slots = [psA[:, 0:512], psA[:, 512:1024], psB[:, 0:512], psB[:, 512:1024]]
            evac_marks = []
            xls = [None, None]
            xls[0] = load_x1q(x1q_bufs[0], 0, 0, lentry)
            for mh in range(2):
                load_attn_w_half(wh, mh, lentry if mh == 0 else [(ct, ct.n)])
                wload = (dG, dG.n)
                for tq in range(NQ):
                    g = mh * NQ + tq
                    xb = g % 2
                    xload = xls[xb]
                    nmh, ntq = (mh, tq + 1) if tq + 1 < NQ else (mh + 1, 0)
                    if nmh < 2:
                        gw = [(ct, ct.n)]
                        xls[1 - xb] = load_x1q(x1q_bufs[1 - xb], 1 - xb, ntq, gw)
                    for cc in range(4):
                        r = tq * 4 + cc
                        for fg in range(2):
                            ps = slots[ci % 4]
                            x1b = x1q_bufs[xb]
                            for k in range(NF):
                                w = []
                                if k == 0:
                                    w = [wload, xload]
                                    if ci < 4:
                                        w += entry
                                    else:
                                        w.append(evac_marks[ci - 4])
                                pr.emit("tensor", (lambda ps, k, cc, fg, x1b: lambda e: e.matmul(
                                    ps, x1b[:, k * QT + cc * 128: k * QT + cc * 128 + 128],
                                    wsb[:, k * 1024 + fg * 512: k * 1024 + (fg + 1) * 512],
                                    start=(k == 0), stop=(k == NF - 1)))(ps, k, cc, fg, x1b),
                                        waits=w, incs=([(ct, 1)] if k == NF - 1 else []))
                            vq = arena[:, r * D + mh * 1024 + fg * 512:
                                       r * D + mh * 1024 + (fg + 1) * 512]
                            ew = entry if ci == 0 else [prev_mark]
                            if evac == "v":
                                pr.emit("scalar", (lambda vq, ps: lambda e: e.copy(vq, ps))(vq, ps),
                                        waits=[ct] + ew, incs=[(ca, 1)])
                                evac_marks.append((ca, ca.n))
                            elif evac == "y":
                                pr.emit("vector", (lambda vq, ps: lambda e: e.tensor_mul(vq, ps, vq))(vq, ps),
                                        waits=[ct] + ew, incs=[(cv, 1)])
                                evac_marks.append((cv, cv.n))
                            else:
                                sw = [ct] + ew + ([sil_mark[0]] if sil_mark[0] else [])
                                pr.emit("scalar", (lambda ps: lambda e: e.activation(
                                    lnscr[:, 0:512], ps, AF.Silu))(ps),
                                        waits=sw, incs=[(ca, 1)])
                                evac_marks.append((ca, ca.n))
                                pr.emit("vector", (lambda vq: lambda e: e.tensor_mul(
                                    vq, vq, lnscr[:, 0:512]))(vq),
                                        waits=[ca] + ew, incs=[(cv, 1)])
                                sil_mark[0] = (cv, cv.n)
                            ci += 1
                            if pump is not None:
                                pump(pn)
            if evac == "g":
                return sil_mark[0]
            return evac_marks[-1]

        a1_mark = proj_v_tm(wv_h, p0_marks[-1], pump1, 1)

        def dbg_dump(nm, _unused=None):
            if not KDBG:
                return
            w = snap()
            DBGM.clear()
            for rr in range(NT):
                pr.emit("sync", (lambda rr: lambda e: e.dma_start(
                    dbg_h[nm][rr * 128:(rr + 1) * 128, :],
                    arena[:, rr * D:(rr + 1) * D]))(rr),
                        waits=(w if rr == 0 else []), incs=[(dDB, 16)])
            DBGM.append((dDB, dDB.n))

        dbg_dump("dbgv")

        a1done = [a1_mark]
        pr.emit("vector", lambda e: e.tensor_scalar(lnl128[:, :], lnl[:, :], 128.0, None, ALU.mult),
                waits=a1done, incs=[(cv, 1)])
        scan_cv = None
        scan_ca = None
        gen_ct = None
        prev2 = [None, None]
        for h in range(H):
            gw = ([ca] if h == 0 else [gen_ct]) + list(DBGM)
            pr.emit("vector", (lambda h: lambda e: e.tensor_scalar(
                tgf[:, :], emat[:, :], lnl[:, h:h + 1], 0.0, ALU.mult, ALU.min))(h),
                    waits=gw + a1done, incs=[(cv, 1)])
            pr.emit("scalar", lambda e: e.activation(sqdump[:, 0:128], tgf[:, :], AF.Exp),
                    waits=[cv], incs=[(ca, 1)])
            pr.emit("vector", lambda e: e.tensor_mul(tgen[:, 0:128], sqdump[:, 0:128], mlo[:, :]),
                    waits=[ca], incs=[(cv, 1)])
            pr.emit("vector", (lambda h: lambda e: e.tensor_scalar(
                tgf[:, :], emat[:, :], lnl[:, h:h + 1], lnl128[:, h:h + 1],
                ALU.mult, ALU.add))(h), waits=[ca], incs=[(cv, 1)])
            pr.emit("scalar", lambda e: e.activation(sqdump[:, 0:128], tgf[:, :], AF.Exp),
                    waits=[cv], incs=[(ca, 1)])
            pr.emit("vector", lambda e: e.tensor_mul(tgen[:, 128:256], sqdump[:, 0:128], mhi[:, :]),
                    waits=[ca], incs=[(cv, 1)])
            tgen_mark = (cv, cv.n)
            pr.emit("vector", lambda e: e.memset(cbuf[:, :], 0.0), incs=[(cv, 1)])
            for r in range(NT):
                psx = psA if r % 2 == 0 else psB
                pq = psx[:, 1024:1280]
                vsl = arena[:, r * D + h * 128: r * D + h * 128 + 128]
                sf = sf32[:, (r % 2) * 128:(r % 2) * 128 + 128]
                w0 = [tgen_mark]
                if prev2[r % 2] is not None:
                    w0.append(prev2[r % 2])
                pr.emit("tensor", (lambda psx, vsl: lambda e: e.matmul(
                    psx[:, 1024:1152], tgen[:, 0:128], vsl, start=True, stop=True))(psx, vsl),
                        waits=w0)
                pr.emit("tensor", (lambda psx, vsl: lambda e: e.matmul(
                    psx[:, 1536:1664], tgen[:, 128:256], vsl, start=True, stop=True))(psx, vsl),
                        incs=[(ct, 1)])
                mmm = (ct, ct.n)
                pr.emit("vector", (lambda sf, psx: lambda e: e.tensor_add(
                    sf, psx[:, 1024:1152], cbuf[:, :]))(sf, psx),
                        waits=[mmm])
                pr.emit("vector", (lambda sf, psx, h: lambda e: e.scalar_tensor_tensor(
                    cbuf[:, :], sf, l128[:, h:h + 1], psx[:, 1536:1664],
                    ALU.mult, ALU.add))(sf, psx, h))
                pr.emit("vector", (lambda vsl, sf: lambda e: e.tensor_copy(vsl, sf))(vsl, sf),
                        incs=[(cv, 1)])
                prev2[r % 2] = (cv, cv.n)
                scan_ca = (cv, cv.n)
            gen_ct = (ct, ct.n)
            pump1(2)
        scan_mark = (cv, cv.n)

        dbg_dump("dbg2")

        a3_mark = proj_v_tm(wq_h, scan_mark, pump1, 2, evac="y")
        dbg_dump("dbg3")

        exhaust1()

        a3done = [a3_mark] + list(DBGM)
        pr.emit("sync", lambda e: e.dma_start(lngr[:, :], lngr_h),
                waits=a3done + [(ct, ct.n)], incs=[(dSM, 16)])
        pr.emit("sync", lambda e: e.dma_start(lnbr[:, :], lnbr_h), incs=[(dSM, 16)])
        gload = (dSM, dSM.n)
        for n in range(4):
            pr.emit("tensor", (lambda n: lambda e: e.matmul(
                psA[:, n * 512:(n + 1) * 512], ones1[:, :],
                lngr[0:1, n * 512:(n + 1) * 512], start=True, stop=True))(n),
                    waits=([gload] + a3done if n == 0 else []))
            pr.emit("tensor", (lambda n: lambda e: e.matmul(
                psB[:, n * 512:(n + 1) * 512], ones1[:, :],
                lnbr[0:1, n * 512:(n + 1) * 512], start=True, stop=True))(n),
                    incs=([(ct, 1)] if n == 3 else []))
        gbcast = (ct, ct.n)
        a4marks = [None, None]
        for r in range(NT):
            ar = arena[:, r * D:(r + 1) * D]
            rs = 0 if r % 2 == 0 else 6
            w0 = (a3done if r < 2 else [a4marks[r % 2]])
            pr.emit("scalar", (lambda ar, rs: lambda e: e.activation(
                sqdump[:, :], ar, AF.Square, accum_out=srow[:, rs:rs + 1]))(ar, rs),
                    waits=w0 + [cv], incs=[(ca, 1)])
            pr.emit("vector", (lambda ar, rs: lambda e: e.reduce_sum(
                srow[:, rs + 1:rs + 2], ar, axis=AX.X))(ar, rs), waits=w0)
            pr.emit("vector", (lambda rs: lambda e: e.tensor_scalar(
                srow[:, rs + 2:rs + 3], srow[:, rs + 1:rs + 2], 1.0 / D, None, ALU.mult))(rs))
            pr.emit("vector", (lambda rs: lambda e: e.tensor_mul(
                srow[:, rs + 3:rs + 4], srow[:, rs + 2:rs + 3], srow[:, rs + 2:rs + 3]))(rs))
            pr.emit("vector", (lambda rs: lambda e: e.scalar_tensor_tensor(
                srow[:, rs + 4:rs + 5], srow[:, rs:rs + 1], 1.0 / D,
                srow[:, rs + 3:rs + 4], ALU.mult, ALU.subtract))(rs),
                    waits=[ca], incs=[(cv, 1)])
            pr.emit("scalar", (lambda rs: lambda e: e.activation(
                srow[:, rs + 5:rs + 6], srow[:, rs + 4:rs + 5], AF.Ln, bias=epsb[:, :]))(rs),
                    waits=[cv])
            pr.emit("scalar", (lambda rs: lambda e: e.activation(
                srow[:, rs + 5:rs + 6], srow[:, rs + 5:rs + 6], AF.Exp, scale=mhalf[:, :]))(rs),
                    incs=[(ca, 1)])
            pr.emit("vector", (lambda ar, rs: lambda e: e.tensor_scalar(
                lnscr[:, :], ar, srow[:, rs + 2:rs + 3], srow[:, rs + 5:rs + 6],
                ALU.subtract, ALU.mult))(ar, rs), waits=[ca])
            pr.emit("vector", lambda e: e.tensor_mul(
                lnscr2[:, :], lnscr[:, :], psA[:, :]), waits=[gbcast])
            pr.emit("vector", (lambda ar: lambda e: e.tensor_add(
                ar, lnscr2[:, :], psB[:, :]))(ar),
                    incs=[(cv, 1)])
            a4marks[r % 2] = (cv, cv.n)
            pump2(2)
        a4_mark = (cv, cv.n)

        dbg_dump("dbg4")

        a5_mark = proj_v_tm(wg_h, a4_mark, pump2, 1, evac="g", lentry=[(ct, ct.n)])
        dbg_dump("dbg5")

        a5done = [a5_mark, (ct, ct.n)]
        for r in range(NT):
            pr.emit("sync", (lambda r: lambda e: e.dma_start(
                x1tm_h[r * 128:(r + 1) * 128, :], arena[:, r * D:(r + 1) * D]))(r),
                    waits=(a5done if r == 0 else []), incs=[(dYW, 16)])
        yw = (dYW, dYW.n)
        for h in range(H):
            pr.emit("sync", (lambda h: lambda e: e.dma_start_transpose(
                arena[:, h * D:(h + 1) * D],
                x1tm_h[0:TOK, h * 128:(h + 1) * 128]))(h),
                    waits=([yw] + list(DBGM) if h == 0 else []), incs=[(dYR, 16)])
        yr = (dYR, dYR.n)
        ev_mark = None
        for mh in range(2):
            load_attn_w_half(wo_h, mh, (a5done + [yr]) if mh == 0 else [(ct, ct.n)])
            wload = (dG, dG.n)
            for ml in range(8):
                m = mh * 8 + ml
                ps = psA if m % 2 == 0 else psB
                for h in range(H):
                    for n in range(4):
                        w = []
                        if h == 0 and n == 0:
                            w = [wload, yr]
                            if ev_mark is not None:
                                w.append(ev_mark)
                        pr.emit("tensor", (lambda ps, ml, h, n: lambda e: e.matmul(
                            ps[:, n * 512:(n + 1) * 512],
                            wsb[:, h * 1024 + ml * 128: h * 1024 + ml * 128 + 128],
                            arena[:, h * D + n * 512: h * D + (n + 1) * 512],
                            start=(h == 0), stop=(h == H - 1)))(ps, ml, h, n),
                                waits=w, incs=([(ct, 1)] if h == H - 1 and n == 3 else []))
                st = a6stage[:, (m % 2) * D:(m % 2 + 1) * D]
                pr.emit("scalar", (lambda st, ps: lambda e: e.copy(st, ps[:, :]))(st, ps),
                        waits=[ct, (dAW, dAW.n)], incs=[(ca, 1)])
                ev_mark = (ca, ca.n)
                pr.emit("sync", (lambda st, m: lambda e: e.dma_start(
                    a_h[m * 128:(m + 1) * 128, :], st))(st, m),
                        waits=[ca], incs=[(dAW, 16)])
                pump2(5)

        a6_ct_saved = ct.n
        a6done = [(dAW, dAW.n), (ct, ct.n), (ca, ca.n)]
        bat = [poolA[:, 0:D], poolA[:, D:2 * D]]
        bxs = [poolA[:, 2 * D:6 * D].bitcast(F32)[:, 0:D],
               poolA[:, 2 * D:6 * D].bitcast(F32)[:, D:2 * D]]
        br1 = [poolA[:, 6 * D:10 * D].bitcast(F32)[:, 0:D],
               poolA[:, 6 * D:10 * D].bitcast(F32)[:, D:2 * D]]
        bx2 = [poolA[:, 10 * D:12 * D][:, 0:D], poolA[:, 10 * D:12 * D][:, D:2 * D]]
        for r in range(NT):
            b = r % 2
            at = bat[b]
            xs = bxs[b]
            r1 = br1[b]
            x2 = bx2[b]
            dL = dL0 if b == 0 else dL1
            w0 = a6done if r == 0 else ([cv, ca] if r >= 2 else [])
            pr.emit("sync", (lambda at, r: lambda e: e.dma_start_transpose(
                at, a_h[0:D, r * 128:(r + 1) * 128]))(at, r), waits=w0, incs=[(dL, 16)])
            pr.emit("sync", (lambda xs, r: lambda e: e.dma_start(
                xs, x_h[r * 128:(r + 1) * 128, :]))(xs, r), incs=[(dL, 16)])
            rs = 0 if b == 0 else 6
            ln_tm_chunk(at, r1, rs, dR1, extra_waits=[dL])
            pr.emit("vector", (lambda r1, xs: lambda e: e.tensor_add(r1, r1, xs))(r1, xs),
                    incs=[(cv, 1)])
            pr.emit("sync", (lambda r1, r: lambda e: e.dma_start(
                r1tm_h[r * 128:(r + 1) * 128, :], r1))(r1, r), waits=[cv], incs=[(dR1, 16)])
            ln_tm_chunk(r1, x2, rs, dX2)
            pr.emit("sync", (lambda x2, r: lambda e: e.dma_start(
                x2tm_h[r * 128:(r + 1) * 128, :], x2))(x2, r), waits=[cv], incs=[(dX2, 16)])
            if r == 7:
                dX2_half0 = dX2.n
            pump2(1)

        exhaust2()
        fin_marks = []
        a6_ct = a6_ct_saved
        a6b_done = [(dR1, dR1.n), (dX2, dX2.n), (cv, cv.n), (ca, ca.n)]

        for half in range(2):
            t0 = half * HT
            if half == 0:
                hw = [(dX2, dX2_half0), a5_mark, (ct, a6_ct)] + list(DBGM)
            else:
                hw = [(dX2, dX2.n), (ct, ct.n)] + list(DBGM)
            for f in range(NF):
                pr.emit("sync", (lambda f, t0: lambda e: e.dma_start_transpose(
                    x2h[:, f * HT:(f + 1) * HT],
                    x2tm_h[t0:t0 + HT, f * 128:(f + 1) * 128]))(f, t0),
                        waits=(hw + [dX2] if f == 0 else []), incs=[(dXH, 16)])
            x2load = (dXH, dXH.n)
            S = [poolB, poolC]
            dUs = [dU, dGS]

            def slab_load(buf, sem, hsrc, j, waits):
                kk = NF if hsrc is not td_h else KG // 128
                cols = hsrc[:, j * KG:(j + 1) * KG] if hsrc is not td_h else hsrc[j * KG:(j + 1) * KG, :]
                pr.emit("sync", (lambda buf, cols, kk: lambda e: e.dma_start(
                    buf[:, 0:kk * (KG if kk == NF else D)].rearrange("p (k n) -> p k n", k=kk),
                    cols.rearrange("(k p) n -> p k n", p=128)))(buf, cols, kk),
                        waits=waits, incs=[(sem, 16)])

            if half == 0:
                slab_load(S[0], dUs[0], tu_h, 0, [(dA, dA.n), (cp, cp.n), (cv, cv.n)])
                slab_load(S[1], dUs[1], tg_h, 0, [])
            else:
                slab_load(S[0], dUs[0], tu_h, 0, [(ct, ct.n)])
                slab_load(S[1], dUs[1], tg_h, 0, [])
            hb_marks = []
            dn_marks = []
            for j in range(NKG):
                ub, gb = S[j % 2], S[(j + 1) % 2]
                usem, gsem = dUs[j % 2], dUs[(j + 1) % 2]
                uload = (usem, usem.n)
                gload = (gsem, gsem.n)
                up_marks = []
                for me in range(KG // 128):
                    pme = psA if me % 2 == 0 else psB
                    gi = len(hb_marks)
                    for k in range(NF):
                        w = []
                        if k == 0:
                            w = [uload, x2load]
                            if me >= 2:
                                w.append(up_marks[me - 2])
                            elif gi >= 2:
                                w += [hb_marks[gi - 2][0], hb_marks[gi - 2][1]]
                            if me == 0 and half > 0 and j == 0:
                                w += [(cv, cv.n), (ca, ca.n)]
                            if me == 0 and dn_marks:
                                w.append(dn_marks[-1])
                        for n in range(2):
                            pr.emit("tensor", (lambda pme, ub, me, k, n: lambda e: e.matmul(
                                pme[:, n * 512:n * 512 + 512],
                                ub[:, k * KG + me * 128: k * KG + (me + 1) * 128],
                                x2h[:, k * HT + n * 512: k * HT + (n + 1) * 512],
                                start=(k == 0), stop=(k == NF - 1)))(pme, ub, me, k, n),
                                    waits=(w if n == 0 else []),
                                    incs=([(ct, 1)] if k == NF - 1 and n == 1 else []))
                    pr.emit("scalar", (lambda pme, me: lambda e: e.activation(
                        hbuf[:, me * HT:(me + 1) * HT], pme[:, 0:HT],
                        AF.Silu, scale=su2b[:, :]))(pme, me),
                            waits=[ct], incs=[(ca, 1)])
                    up_marks.append((ca, ca.n))
                slab_load(ub, usem, td_h, j, [(ct, ct.n)])
                dload = (usem, usem.n)
                for me in range(KG // 128):
                    pme = psA if me % 2 == 0 else psB
                    gi = len(hb_marks)
                    for k in range(NF):
                        w = []
                        if k == 0:
                            w = [gload, x2load]
                            if gi >= 2:
                                w.append(hb_marks[gi - 2][1])
                        for n in range(2):
                            pr.emit("tensor", (lambda pme, gb, me, k, n: lambda e: e.matmul(
                                pme[:, HT + n * 512: HT + n * 512 + 512],
                                gb[:, k * KG + me * 128: k * KG + (me + 1) * 128],
                                x2h[:, k * HT + n * 512: k * HT + (n + 1) * 512],
                                start=(k == 0), stop=(k == NF - 1)))(pme, gb, me, k, n),
                                    waits=(w if n == 0 else []),
                                    incs=([(ct, 1)] if k == NF - 1 and n == 1 else []))
                    pr.emit("vector", (lambda pme, me: lambda e: e.tensor_mul(
                        hbuf[:, me * HT:(me + 1) * HT], hbuf[:, me * HT:(me + 1) * HT],
                        pme[:, HT:2 * HT]))(pme, me),
                            waits=[ct, up_marks[me]], incs=[(cv, 1)])
                    hb_marks.append(((ca, ca.n), (cv, cv.n)))
                if j + 1 < NKG:
                    slab_load(gb, gsem, tu_h, j + 1, [(ct, ct.n)])
                hlast = hb_marks[-1]
                for md in range(NF):
                    ps = (psA if md % 2 == 0 else psB)[:, 0:HT]
                    di = len(dn_marks)
                    for k in range(KG // 128):
                        w = []
                        if k == 0:
                            w = [dload, hlast[1]]
                            if di >= 2:
                                w.append(dn_marks[di - 2])
                        for n in range(2):
                            pr.emit("tensor", (lambda ps, ub, md, k, n: lambda e: e.matmul(
                                ps[:, n * 512:(n + 1) * 512],
                                ub[:, k * D + md * 128: k * D + (md + 1) * 128],
                                hbuf[:, k * HT + n * 512: k * HT + (n + 1) * 512],
                                start=(k == 0), stop=(k == KG // 128 - 1)))(ps, ub, md, k, n),
                                    waits=(w if n == 0 else []),
                                    incs=([(ct, 1)] if (k == KG // 128 - 1 and n == 1) else []))
                    acs = accf[:, md * HT:(md + 1) * HT]
                    if j == 0:
                        aw = [ct] + (fin_marks if (half == 1 and md == 0) else []) + \
                             (a6b_done if (half == 0 and md == 0) else [])
                        pr.emit("vector", (lambda acs, ps: lambda e: e.tensor_copy(acs, ps))(acs, ps),
                                waits=aw, incs=[(cv, 1)])
                    else:
                        pr.emit("vector", (lambda acs, ps: lambda e: e.tensor_add(acs, acs, ps))(acs, ps),
                                waits=[ct], incs=[(cv, 1)])
                    dn_marks.append((cv, cv.n))
                if j + 1 < NKG:
                    slab_load(ub, usem, tg_h, j + 1, [(ct, ct.n)])
            ffdone = snap()
            for m in range(NF):
                st = dstage[:, m * HT:(m + 1) * HT]
                pr.emit("scalar", (lambda st, m: lambda e: e.copy(
                    st, accf[:, m * HT:(m + 1) * HT]))(st, m),
                        waits=(ffdone if m == 0 else []), incs=[(ca, 1)])
                pr.emit("sync", (lambda st, m, t0: lambda e: e.dma_start(
                    d_scr_h[m * 128:(m + 1) * 128, t0:t0 + HT], st))(st, m, t0),
                        waits=[ca], incs=[(dDW, 16)])
            cast_mark = (ca, ca.n)
            fdT = [poolA[:, 0:D], poolA[:, D:2 * D]]
            fr1 = [poolA[:, 2 * D:6 * D].bitcast(F32)[:, 0:D],
                   poolA[:, 2 * D:6 * D].bitcast(F32)[:, D:2 * D]]
            foc = [poolA[:, 6 * D:10 * D].bitcast(F32)[:, 0:D],
                   poolA[:, 6 * D:10 * D].bitcast(F32)[:, D:2 * D]]
            for c in range(NT // 2):
                r = half * (NT // 2) + c
                b = c % 2
                dt_ = fdT[b]
                r1 = fr1[b]
                oc = foc[b]
                w0 = [(dDW, dDW.n), (dR1, dR1.n), cast_mark, (ct, ct.n)] if c == 0 else                     ([cv, ca] if c >= 2 else [])
                pr.emit("sync", (lambda dt_, r: lambda e: e.dma_start_transpose(
                    dt_, d_scr_h[0:D, r * 128:(r + 1) * 128]))(dt_, r),
                        waits=w0, incs=[(dL0 if b == 0 else dL1, 16)])
                pr.emit("sync", (lambda r1, r: lambda e: e.dma_start(
                    r1, r1tm_h[r * 128:(r + 1) * 128, :]))(r1, r),
                        incs=[(dL0 if b == 0 else dL1, 16)])
                rs = 0 if b == 0 else 6
                ln_tm_chunk(dt_, oc, rs, dOU, extra_waits=[dL0 if b == 0 else dL1])
                pr.emit("vector", (lambda oc, r1: lambda e: e.tensor_add(oc, oc, r1))(oc, r1),
                        incs=[(cv, 1)])
                pr.emit("sync", (lambda oc, r: lambda e: e.dma_start(
                    out_h[r * 128:(r + 1) * 128, :], oc))(oc, r), waits=[cv], incs=[(dOU, 16)])
            fin_marks = [(dOU, dOU.n), (cv, cv.n), (ca, ca.n)]

        fins = snap()
        pr.emit("gpsimd", lambda e: e.nop(), waits=fins)

        @block.sync
        def _(eng):
            pr.run_engine("sync", eng)

        @block.gpsimd
        def _(eng):
            pr.run_engine("gpsimd", eng)

        @block.vector
        def _(eng):
            pr.run_engine("vector", eng)

        @block.scalar
        def _(eng):
            pr.run_engine("scalar", eng)

        @block.tensor
        def _(eng):
            pr.run_engine("tensor", eng)

    return nc


def make_core_inputs(L0_attn_x, L0_attn_Wq, L0_attn_Wv, L0_attn_Wg, L0_attn_Wo,
                     L0_attn_beta, L0_attn_ln_g, L0_attn_ln_b,
                     L0_ffn_W_up, L0_ffn_W_gate, L0_ffn_W_down):
    common = {
        "wq": np.ascontiguousarray(L0_attn_Wq, dtype=np.float32),
        "wv": np.ascontiguousarray(L0_attn_Wv, dtype=np.float32),
        "wg": np.ascontiguousarray(L0_attn_Wg, dtype=np.float32),
        "wo": np.ascontiguousarray(L0_attn_Wo, dtype=np.float32),
        "beta": np.ascontiguousarray(np.asarray(L0_attn_beta, dtype=np.float32).reshape(1, H)),
        "lngr": np.ascontiguousarray(np.asarray(L0_attn_ln_g, dtype=np.float32).reshape(1, D)),
        "lnbr": np.ascontiguousarray(np.asarray(L0_attn_ln_b, dtype=np.float32).reshape(1, D)),
        "wuT": np.ascontiguousarray(np.asarray(L0_ffn_W_up, dtype=np.float32).T),
        "wgT": np.ascontiguousarray(np.asarray(L0_ffn_W_gate, dtype=np.float32).T),
        "wdT": np.ascontiguousarray(np.asarray(L0_ffn_W_down, dtype=np.float32).T),
        "emat": np.ascontiguousarray(
            np.arange(128, dtype=np.float32)[None, :] - np.arange(128, dtype=np.float32)[:, None]),
        "mlo": np.ascontiguousarray(
            (np.arange(128)[None, :] >= np.arange(128)[:, None]).astype(ml_dtypes.bfloat16)),
        "mhi": np.ascontiguousarray(
            (np.arange(128)[None, :] < np.arange(128)[:, None]).astype(ml_dtypes.bfloat16)),
    }
    xs = np.ascontiguousarray(np.asarray(L0_attn_x, dtype=np.float32))
    in_maps = []
    for b in range(NCORES):
        m = dict(common)
        m["x"] = np.ascontiguousarray(xs[b])
        in_maps.append(m)
    return in_maps


def kernel(**inputs):
    in_maps = make_core_inputs(**inputs)
    nc = build_program()
    res = run_bass_kernel_spmd(nc, in_maps, core_ids=list(range(NCORES)))
    out = np.stack([np.asarray(r["out"]) for r in res.results], axis=0)
    return out.astype(np.float32)
